# version 07
# baseline (speedup 1.0000x reference)
import numpy as np
import ml_dtypes
from contextlib import ExitStack
import concourse.bass as bass
import concourse.mybir as mybir
from concourse.bass_utils import run_bass_kernel_spmd

F32 = mybir.dt.float32
BF16 = mybir.dt.bfloat16
AF = mybir.ActivationFunctionType
ALU = mybir.AluOpType
AX = mybir.AxisListType
bfdt = ml_dtypes.bfloat16

D = 4096
NDC = 32
TALL = 4096
TOWN = 2048
NH = 32
HD = 128
EPS = 1e-5
ISQ = float(1.0 / np.sqrt(128.0))
EXPC = 4.0


class S:
    def __init__(self, h):
        self.h = h
        self.n = 0


class P:
    def __init__(self, nc):
        self.nc = nc
        self.ops = {k: [] for k in "tvags"}
        self.nbar = 0

    def w(self, eng, sem, val):
        if val > 0:
            self.ops[eng].append(lambda e, h=sem.h, v=val: e.wait_ge(h, v))

    def do(self, eng, fn, inc=None, k=1):
        if inc is None:
            self.ops[eng].append(fn)
            return None
        inc.n += k
        self.ops[eng].append(lambda e, fn=fn, h=inc.h, k=k: fn(e).then_inc(h, k))
        return inc.n

    def barrier(self, bar):
        self.nbar += 1
        tgt = 5 * self.nbar
        for eng in "tvags":
            bar.n += 1
            if eng == "s":
                self.ops[eng].append(lambda e, h=bar.h: e.sem_inc(h, 1))
            else:
                self.ops[eng].append(lambda e, h=bar.h: e.drain().then_inc(h, 1))
            self.ops[eng].append(lambda e, h=bar.h, t=tgt: e.wait_ge(h, t))

    def flush(self):
        with self.nc.Block() as blk:
            blk.tensor(lambda e: [f(e) for f in self.ops["t"]])
            blk.vector(lambda e: [f(e) for f in self.ops["v"]])
            blk.scalar(lambda e: [f(e) for f in self.ops["a"]])
            blk.gpsimd(lambda e: [f(e) for f in self.ops["g"]])
            blk.sync(lambda e: [f(e) for f in self.ops["s"]])


def rearr_dT(ap, t0, tw):
    return ap.rearrange("(dc p) t -> p dc t", p=128)[:, :, t0:t0 + tw]


def bcast_dc(ap2d, ndc=NDC):
    a = ap2d
    return bass.AP(a.tensor, a.offset, [list(a.ap[0]), [0, ndc], list(a.ap[1])])


def build(dump=(), paranoid=False, skip_attn=False):
    nc = bass.Bass("TRN2", target_bir_lowering=False)

    xT = nc.dram_tensor("xT", [D, TALL], F32, kind="ExternalInput").ap()
    wqkv = nc.dram_tensor("wqkv", [D, 3 * D], F32, kind="ExternalInput").ap()
    ow = nc.dram_tensor("ow", [D, D], F32, kind="ExternalInput").ap()
    wrT = nc.dram_tensor("wrT", [D, 64], F32, kind="ExternalInput").ap()
    aT = nc.dram_tensor("aT", [D, 1024], F32, kind="ExternalInput").ap()
    bfm = nc.dram_tensor("bfm", [1024, D], F32, kind="ExternalInput").ap()
    qkb = nc.dram_tensor("qkb", [128, 96], F32, kind="ExternalInput").ap()
    obr = nc.dram_tensor("obr", [128, 32], F32, kind="ExternalInput").ap()
    emat = nc.dram_tensor("emat", [64, 1024], F32, kind="ExternalInput").ap()
    idf = nc.dram_tensor("idf", [128, 128], F32, kind="ExternalInput").ap()

    outT = nc.dram_tensor("outT", [D, TOWN], F32, kind="ExternalOutput").ap()

    wqkv_bf = nc.dram_tensor("wqkv_bf", [D, 3 * D], BF16, kind="Internal").ap()
    ow_bf = nc.dram_tensor("ow_bf", [D, D], BF16, kind="Internal").ap()
    aT_bf = nc.dram_tensor("aT_bf", [D, 1024], BF16, kind="Internal").ap()
    bf_bf = nc.dram_tensor("bf_bf", [1024, D], BF16, kind="Internal").ap()
    x1T = nc.dram_tensor("x1T", [D, TALL], F32, kind="Internal").ap()
    x2T = nc.dram_tensor("x2T", [D, TOWN], F32, kind="Internal").ap()
    kbuf = nc.dram_tensor("kbuf", [NH, HD, TALL], BF16, kind="Internal").ap()
    vbufT = nc.dram_tensor("vbufT", [NH, HD, TALL], BF16, kind="Internal").ap()
    qbuf = nc.dram_tensor("qbuf", [NH, HD, TOWN], BF16, kind="Internal").ap()

    dumps = {}
    if "x1" in dump:
        dumps["d_x1"] = nc.dram_tensor("d_x1", [D, TALL], F32, kind="ExternalOutput").ap()
    if "kvq" in dump:
        dumps["d_k"] = nc.dram_tensor("d_k", [NH, HD, TALL], BF16, kind="ExternalOutput").ap()
        dumps["d_v"] = nc.dram_tensor("d_v", [NH, HD, TALL], BF16, kind="ExternalOutput").ap()
        dumps["d_q"] = nc.dram_tensor("d_q", [NH, HD, TOWN], BF16, kind="ExternalOutput").ap()
    if "x2" in dump:
        dumps["d_x2"] = nc.dram_tensor("d_x2", [D, TOWN], F32, kind="ExternalOutput").ap()

    es = ExitStack()
    with es:
        ones_bf = es.enter_context(nc.sbuf_tensor("ones_bf", [128, 1], BF16))
        ones_f = es.enter_context(nc.sbuf_tensor("ones_f", [128, 1], F32))
        ones1 = es.enter_context(nc.sbuf_tensor("ones1", [1, 128], F32))
        wr_sb = es.enter_context(nc.sbuf_tensor("wr_sb", [128, NDC, 64], BF16))
        e_sb = es.enter_context(nc.sbuf_tensor("e_sb", [64, 1024], F32))
        id_sb = es.enter_context(nc.sbuf_tensor("id_sb", [128, 128], F32))
        qkb_sb = es.enter_context(nc.sbuf_tensor("qkb_sb", [128, 96], F32))
        id_bf = es.enter_context(nc.sbuf_tensor("id_bf", [128, 128], BF16))
        ob_sb = es.enter_context(nc.sbuf_tensor("ob_sb", [128, 32], F32))
        epsb = es.enter_context(nc.sbuf_tensor("epsb", [1, 1], F32))
        cexp = es.enter_context(nc.sbuf_tensor("cexp", [128, 1], F32))
        rows = es.enter_context(nc.sbuf_tensor("rows", [1, 8, 128], F32))
        m8a = es.enter_context(nc.sbuf_tensor("m8a", [128, 8], F32))
        m8b = es.enter_context(nc.sbuf_tensor("m8b", [128, 8], F32))
        ntau = es.enter_context(nc.sbuf_tensor("ntau", [128, 1], F32))
        mrs = es.enter_context(nc.sbuf_tensor("mrs", [128, 64], F32))
        etile = es.enter_context(nc.sbuf_tensor("etile", [128, 64], F32))
        mask = es.enter_context(nc.sbuf_tensor("mask", [128, 64], F32))
        gsum = es.enter_context(nc.sbuf_tensor("gsum", [128, 1], F32))
        grec = es.enter_context(nc.sbuf_tensor("grec", [128, 1], F32))
        gtile = es.enter_context(nc.sbuf_tensor("gtile", [128, 64], F32))

        PS = es.enter_context(nc.psum_tensor("PS", [128, 4096], F32))

        def bank(i, n=1):
            return PS[:, 512 * i:512 * (i + n)]

        tm = S(es.enter_context(nc.semaphore("tm")))
        vm = S(es.enter_context(nc.semaphore("vm")))
        am = S(es.enter_context(nc.semaphore("am")))
        bar = S(es.enter_context(nc.semaphore("bar")))
        cst = S(es.enter_context(nc.semaphore("cst")))
        lnx = S(es.enter_context(nc.semaphore("lnx")))
        dmp = S(es.enter_context(nc.semaphore("dmp")))
        c_ = [S(es.enter_context(nc.semaphore(f"c{i}"))) for i in range(2)]
        sc = [S(es.enter_context(nc.semaphore(f"sc{i}"))) for i in range(2)]
        fs = [S(es.enter_context(nc.semaphore(f"fs{i}"))) for i in range(2)]
        xr = [S(es.enter_context(nc.semaphore(f"xr{i}"))) for i in range(2)]
        sx = [S(es.enter_context(nc.semaphore(f"sx{i}"))) for i in range(2)]
        ws = [S(es.enter_context(nc.semaphore(f"ws{i}"))) for i in range(2)]
        sk = [S(es.enter_context(nc.semaphore(f"sk{i}"))) for i in range(2)]
        k3s = S(es.enter_context(nc.semaphore("k3s")))
        v3s = S(es.enter_context(nc.semaphore("v3s")))
        q3s = S(es.enter_context(nc.semaphore("q3s")))
        wo = [S(es.enter_context(nc.semaphore(f"wo{i}"))) for i in range(2)]
        xl = [S(es.enter_context(nc.semaphore(f"xl{i}"))) for i in range(2)]
        s2 = [S(es.enter_context(nc.semaphore(f"s2{i}"))) for i in range(2)]

        p = P(nc)

        ln_free = [0]
        ln_banks_free = [0]
        src_ready = [[]]
        scoresT_free = [0]
        slab_consumed = [0, 0]
        gexp_evac = [0]
        gx_read = [0]
        gT_read = [0]
        hg_free = [0, 0]
        res_done = [0, 0]
        out_stored = [0, 0]
        buf = {}

        def emit_ln(src, t0, xnbf, off):
            xf, xs2, xb2, xq2 = buf["xf"], buf["xs2"], buf["xb2"], buf["xq2"]
            for (s_, v_) in src_ready[0]:
                p.w("s", s_, v_)
            for sci in range(4):
                tt0 = t0 + 128 * sci
                p.w("s", vm, ln_free[0])
                p.do("s", lambda e, src=src, tt0=tt0, xf=xf:
                     e.dma_start(out=xf[:, :, :], in_=rearr_dT(src, tt0, 128)), inc=lnx, k=16)
                p.w("v", lnx, lnx.n)
                p.do("v", lambda e, xf=xf, xb2=xb2: e.tensor_copy(out=xb2[:, :, :], in_=xf[:, :, :]), inc=vm)
                p.w("v", vm, vm.n)
                p.do("v", lambda e, xb2=xb2, xq2=xq2:
                     e.tensor_mul(out=xq2[:, :, :], in0=xb2[:, :, :], in1=xb2[:, :, :]), inc=vm)
                p.w("t", vm, vm.n)
                p.w("t", vm, ln_banks_free[0])
                for dc in range(NDC):
                    p.do("t", lambda e, dc=dc, xb2=xb2:
                         e.matmul(bank(6)[0:1, 0:128], ones_bf[:, :], xb2[:, dc, :],
                                  start=(dc == 0), stop=(dc == NDC - 1)))
                for dc in range(NDC):
                    p.do("t", lambda e, dc=dc, xq2=xq2:
                         e.matmul(bank(7)[0:1, 0:128], ones_bf[:, :], xq2[:, dc, :],
                                  start=(dc == 0), stop=(dc == NDC - 1)),
                         inc=(tm if dc == NDC - 1 else None))
                p.w("v", tm, tm.n)
                p.do("v", lambda e: e.tensor_scalar_mul(out=rows[0:1, 0, :], in0=bank(6)[0:1, 0:128], scalar1=1.0 / D))
                p.do("v", lambda e: e.tensor_scalar_mul(out=rows[0:1, 1, :], in0=bank(7)[0:1, 0:128], scalar1=1.0 / D), inc=vm)
                p.w("v", vm, vm.n)
                p.do("v", lambda e: e.tensor_mul(out=rows[0:1, 2, :], in0=rows[0:1, 0, :], in1=rows[0:1, 0, :]), inc=vm)
                p.w("v", vm, vm.n)
                p.do("v", lambda e: e.tensor_sub(out=rows[0:1, 3, :], in0=rows[0:1, 1, :], in1=rows[0:1, 2, :]), inc=vm)
                p.w("a", vm, vm.n)
                p.do("a", lambda e: e.activation(out=rows[0:1, 4, :], in_=rows[0:1, 3, :], func=AF.Ln, bias=epsb[0:1, :]), inc=am)
                p.w("a", am, am.n)
                p.do("a", lambda e: e.activation(out=rows[0:1, 5, :], in_=rows[0:1, 4, :], func=AF.Exp, scale=-0.5), inc=am)
                p.w("t", am, am.n)
                p.do("t", lambda e: e.matmul(bank(6)[:, 0:128], ones1[0:1, :], rows[0:1, 0, :], start=True, stop=True))
                p.do("t", lambda e: e.matmul(bank(7)[:, 0:128], ones1[0:1, :], rows[0:1, 5, :], start=True, stop=True), inc=tm)
                p.w("v", tm, tm.n)
                p.do("v", lambda e, xf=xf, xs2=xs2:
                     e.tensor_tensor(out=xs2[:, :, :], in0=xf[:, :, :],
                                     in1=bcast_dc(bank(6)[:, 0:128]), op=ALU.subtract), inc=vm)
                ln_free[0] = vm.n
                p.w("v", vm, vm.n)
                p.do("v", lambda e, xnbf=xnbf, off=off, sc=sci, xs2=xs2:
                     e.tensor_tensor(out=xnbf[:, :, off + 128 * sc:off + 128 * (sc + 1)],
                                     in0=xs2[:, :, :], in1=bcast_dc(bank(7)[:, 0:128]),
                                     op=ALU.mult), inc=vm)
                ln_banks_free[0] = vm.n
                if paranoid:
                    p.barrier(bar)

        def emit_ffn(xnbf, t0, resid_src, dst, dst_t0):
            hg, fslab, xres, xout = buf["hg"], buf["fslab"], buf["xres"], buf["xout"]
            p.w("t", vm, vm.n)
            p.w("t", am, scoresT_free[0])
            for dc in range(NDC):
                p.do("t", lambda e, dc=dc, xnbf=xnbf:
                     e.matmul(bank(3)[0:64, :], wr_sb[:, dc, :], xnbf[:, dc, :],
                              start=(dc == 0), stop=(dc == NDC - 1)),
                     inc=(tm if dc == NDC - 1 else None))
            p.w("a", tm, tm.n)
            p.do("a", lambda e: e.activation(out=buf['sT_sb'][:, :], in_=bank(3)[0:64, :], func=AF.Identity), inc=am)
            scoresT_free[0] = am.n
            p.w("t", am, am.n)
            for tt in range(4):
                p.do("t", lambda e, tt=tt: e.transpose(bank(4)[:, 64 * tt:64 * (tt + 1)],
                                                       buf['sT_sb'][:, 128 * tt:128 * (tt + 1)], id_sb[0:64, 0:64]),
                     inc=(tm if tt == 3 else None))
            p.w("a", tm, tm.n)
            p.do("a", lambda e: e.activation(out=buf['snat'][:, :, :], in_=bank(4)[:, 0:256], func=AF.Identity), inc=am)
            for tt in range(4):
                p.w("v", am, am.n)
                p.do("v", lambda e, tt=tt: e.max(out=m8a[:, :], in_=buf['snat'][:, tt, :]), inc=vm)
                p.w("v", vm, vm.n)
                p.do("v", lambda e, tt=tt: e.match_replace(out=mrs[:, :], in_to_replace=m8a[:, :],
                                                           in_values=buf['snat'][:, tt, :], imm_value=-1e30), inc=vm)
                p.w("v", vm, vm.n)
                p.do("v", lambda e: e.max(out=m8b[:, :], in_=mrs[:, :]), inc=vm)
                p.w("v", vm, vm.n)
                p.do("v", lambda e: e.tensor_scalar_mul(out=ntau[:, :], in0=m8b[:, 7:8], scalar1=-1.0), inc=vm)
                p.w("a", vm, vm.n)
                p.do("a", lambda e, tt=tt: e.activation(out=etile[:, :], in_=buf['snat'][:, tt, :], func=AF.Exp,
                                                        bias=ntau[:, :]), inc=am)
                p.w("v", am, am.n)
                p.do("v", lambda e, tt=tt: e.tensor_scalar(out=mask[:, :], in0=buf['snat'][:, tt, :],
                                                           scalar1=m8b[:, 7:8], scalar2=None, op0=ALU.is_ge), inc=vm)
                p.w("v", vm, vm.n)
                p.do("v", lambda e: e.tensor_mul(out=etile[:, :], in0=etile[:, :], in1=mask[:, :]), inc=vm)
                p.w("v", vm, vm.n)
                p.do("v", lambda e: e.reduce_sum(out=gsum[:, :], in_=etile[:, :], axis=AX.X), inc=vm)
                p.w("v", vm, vm.n)
                p.do("v", lambda e: e.reciprocal(out=grec[:, :], in_=gsum[:, :]), inc=vm)
                p.w("v", vm, vm.n)
                p.w("v", tm, gT_read[0])
                p.do("v", lambda e: e.tensor_scalar(out=gtile[:, :], in0=etile[:, :], scalar1=grec[:, :],
                                                    scalar2=None, op0=ALU.mult), inc=vm)
                p.w("t", vm, vm.n)
                p.do("t", lambda e, tt=tt: e.transpose(bank(5)[0:64, 128 * tt:128 * (tt + 1)],
                                                       gtile[:, :], id_sb[:, :]), inc=tm)
                gT_read[0] = tm.n
            p.w("a", tm, tm.n)
            p.do("a", lambda e: e.activation(out=buf['GT_sb'][:, :], in_=bank(5)[0:64, :], func=AF.Identity), inc=am)
            gt_ready = am.n

            for jp in range(4):
                pa = jp % 2
                p.w("s", tm, slab_consumed[pa])
                p.do("s", lambda e, jp=jp, pa=pa, fslab=fslab:
                     e.dma_start(out=fslab[pa][:, 0:NDC * 256].rearrange("p (dc c) -> p dc c", c=256),
                                 in_=aT_bf.rearrange("(dc p) j -> p dc j", p=128)[:, :, 256 * jp:256 * (jp + 1)]),
                     inc=fs[pa], k=16)
                p.w("t", fs[pa], fs[pa].n)
                p.w("t", am, gt_ready)
                for j01 in range(2):
                    j = 2 * jp + j01
                    p.w("t", vm, hg_free[j01])
                    for dc in range(NDC):
                        p.do("t", lambda e, dc=dc, j01=j01, pa=pa, fslab=fslab, xnbf=xnbf:
                             e.matmul(bank(j01),
                                      fslab[pa][:, 0:NDC * 256].rearrange("p (dc c) -> p dc c", c=256)[:, dc, 128 * j01:128 * (j01 + 1)],
                                      xnbf[:, dc, :], start=(dc == 0), stop=(dc == NDC - 1)))
                    p.w("t", am, gexp_evac[0])
                    p.do("t", lambda e, j=j: e.matmul(bank(2), e_sb[:, 128 * j:128 * (j + 1)], buf['GT_sb'][:, :],
                                                      start=True, stop=True), inc=tm)
                    slab_consumed[pa] = tm.n
                    p.w("a", tm, tm.n)
                    p.w("a", vm, gx_read[0])
                    p.do("a", lambda e: e.activation(out=buf['gx_sb'][:, :], in_=bank(2), func=AF.Identity), inc=am)
                    gexp_evac[0] = am.n
                    p.w("v", am, am.n)
                    p.do("v", lambda e, j=j, j01=j01, hg=hg:
                         e.tensor_tensor(out=hg[:, j, :], in0=bank(j01), in1=buf['gx_sb'][:, :], op=ALU.mult), inc=vm)
                    hg_free[j01] = vm.n
                    gx_read[0] = vm.n
            hg_all = vm.n

            for ctg in range(4):
                pa = ctg % 2
                p.w("s", tm, slab_consumed[pa])
                p.do("s", lambda e, ctg=ctg, pa=pa, fslab=fslab:
                     e.dma_start(out=fslab[pa][:, :].rearrange("p (j c) -> p j c", c=1024),
                                 in_=bf_bf.rearrange("(j p) c -> p j c", p=128)[:, :, 1024 * ctg:1024 * (ctg + 1)]),
                     inc=fs[pa], k=16)
                p.w("t", fs[pa], fs[pa].n)
                p.w("t", vm, hg_all)
                for ct8 in range(8):
                    ct = 8 * ctg + ct8
                    pb = ct8 % 2
                    p.w("t", vm, res_done[pb])
                    for j in range(8):
                        p.do("t", lambda e, j=j, ct8=ct8, pa=pa, pb=pb, fslab=fslab, hg=hg:
                             e.matmul(bank(pb),
                                      fslab[pa][:, :].rearrange("p (j c) -> p j c", c=1024)[:, j, 128 * ct8:128 * (ct8 + 1)],
                                      hg[:, j, :], start=(j == 0), stop=(j == 7)),
                             inc=(tm if j == 7 else None))
                    slab_consumed[pa] = tm.n
                    p.w("s", vm, res_done[pb])
                    p.do("s", lambda e, ct=ct, t0=t0, pb=pb, resid_src=resid_src, xres=xres:
                         e.dma_start(out=xres[pb][:, :], in_=resid_src[128 * ct:128 * (ct + 1), t0:t0 + 512]),
                         inc=xr[pb], k=16)
                    p.w("v", tm, tm.n)
                    p.w("v", xr[pb], xr[pb].n)
                    p.w("v", sx[pb], out_stored[pb])
                    p.do("v", lambda e, pb=pb, xres=xres, xout=xout:
                         e.tensor_tensor(out=xout[pb][:, :], in0=bank(pb), in1=xres[pb][:, :], op=ALU.add), inc=vm)
                    res_done[pb] = vm.n
                    p.w("a", vm, vm.n)
                    p.do("a", lambda e, ct=ct, dst=dst, dst_t0=dst_t0, pb=pb, xout=xout:
                         e.dma_start(out=dst[128 * ct:128 * (ct + 1), dst_t0:dst_t0 + 512], in_=xout[pb][:, :]),
                         inc=sx[pb], k=16)
                    out_stored[pb] = sx[pb].n
            p.w("a", sx[0], sx[0].n)
            p.w("a", sx[1], sx[1].n)
            if paranoid:
                p.barrier(bar)

        p.do("v", lambda e: e.memset(ones_bf[:, :], 1.0))
        p.do("v", lambda e: e.memset(ones_f[:, :], 1.0))
        p.do("v", lambda e: e.memset(ones1[:, :], 1.0))
        p.do("v", lambda e: e.memset(epsb[:, :], EPS))
        p.do("v", lambda e: e.memset(cexp[:, :], -EXPC), inc=vm)
        p.do("s", lambda e: e.dma_start(out=e_sb[:, :], in_=emat), inc=cst, k=16)
        p.do("s", lambda e: e.dma_start(out=id_sb[:, :], in_=idf), inc=cst, k=16)
        p.do("s", lambda e: e.dma_start(out=qkb_sb[:, :], in_=qkb), inc=cst, k=16)
        p.do("s", lambda e: e.dma_start(out=ob_sb[:, :], in_=obr), inc=cst, k=16)
        p.w("v", cst, cst.n)
        p.do("v", lambda e: e.tensor_copy(out=id_bf[:, :], in_=id_sb[:, :]), inc=vm)

        if True:
            with ExitStack() as p0:
                cf = [p0.enter_context(nc.sbuf_tensor(f"cf{i}", [128, 3 * D], F32)) for i in range(2)]
                cb = [p0.enter_context(nc.sbuf_tensor(f"cb{i}", [128, 3 * D], BF16)) for i in range(2)]
                conv = [(wqkv, wqkv_bf, 32, 3 * D), (ow, ow_bf, 32, D),
                        (aT, aT_bf, 32, 1024), (bfm, bf_bf, 8, D), (wrT, None, 32, 64)]
                last_cvt = [0, 0]
                last_st = [0, 0]
                for (src, dst, nsl, cols) in conv:
                    for i in range(nsl):
                        pa = i % 2
                        p.w("s", vm, last_cvt[pa])
                        p.do("s", lambda e, src=src, i=i, cols=cols, pa=pa:
                             e.dma_start(out=cf[pa][:, 0:cols], in_=src[128 * i:128 * (i + 1), :]), inc=c_[pa], k=16)
                        p.w("v", c_[pa], c_[pa].n)
                        p.w("v", sc[pa], last_st[pa])
                        if dst is None:
                            p.do("v", lambda e, i=i, pa=pa:
                                 e.tensor_copy(out=wr_sb[:, i, :], in_=cf[pa][:, 0:64]), inc=vm)
                            last_cvt[pa] = vm.n
                        else:
                            p.do("v", lambda e, pa=pa, cols=cols:
                                 e.tensor_copy(out=cb[pa][:, 0:cols], in_=cf[pa][:, 0:cols]), inc=vm)
                            last_cvt[pa] = vm.n
                            p.w("a", vm, vm.n)
                            p.do("a", lambda e, dst=dst, i=i, cols=cols, pa=pa:
                                 e.dma_start(out=dst[128 * i:128 * (i + 1), :], in_=cb[pa][:, 0:cols]),
                                 inc=sc[pa], k=16)
                            last_st[pa] = sc[pa].n
                p.w("a", sc[0], sc[0].n)
                p.w("a", sc[1], sc[1].n)
                conv_done = (sc[0].n, sc[1].n)
                p.barrier(bar)

            with ExitStack() as p1:
                buf["xf"] = p1.enter_context(nc.sbuf_tensor("xf", [128, NDC, 128], F32))
                buf["xs2"] = p1.enter_context(nc.sbuf_tensor("xs2", [128, NDC, 128], F32))
                buf["xb2"] = p1.enter_context(nc.sbuf_tensor("xb2", [128, NDC, 128], BF16))
                buf["xq2"] = p1.enter_context(nc.sbuf_tensor("xq2", [128, NDC, 128], BF16))
                buf["hg"] = p1.enter_context(nc.sbuf_tensor("hg", [128, 8, 512], BF16))
                buf["fslab"] = [p1.enter_context(nc.sbuf_tensor(f"fslab{i}", [128, 8192], BF16)) for i in range(2)]
                buf["xres"] = [p1.enter_context(nc.sbuf_tensor(f"xres{i}", [128, 512], F32)) for i in range(2)]
                buf["xout"] = [p1.enter_context(nc.sbuf_tensor(f"xout{i}", [128, 512], F32)) for i in range(2)]
                buf["snat"] = p1.enter_context(nc.sbuf_tensor("snat", [128, 4, 64], F32))
                buf["sT_sb"] = p1.enter_context(nc.sbuf_tensor("sT_sb", [64, 512], F32))
                buf["GT_sb"] = p1.enter_context(nc.sbuf_tensor("GT_sb", [64, 512], F32))
                buf["gx_sb"] = p1.enter_context(nc.sbuf_tensor("gx_sb", [128, 512], F32))
                xn2 = p1.enter_context(nc.sbuf_tensor("xn2", [128, NDC, 1024], BF16))
                xn1 = xn2[:, :, 512:1024]
                wsl = [p1.enter_context(nc.sbuf_tensor(f"wsl{i}", [128, NDC, 128], BF16)) for i in range(2)]
                kst = [p1.enter_context(nc.sbuf_tensor(f"kst{i}", [128, 1024], BF16)) for i in range(2)]

                wsl_consumed = [0, 0]
                qk_banks_free = [0, 0]
                kst_stage_free = [0, 0]

                p.w("s", sc[0], conv_done[0])
                p.w("s", sc[1], conv_done[1])
                p.w("s", cst, cst.n)
                for W in range(4):
                    for half in range(2):
                        t0 = 1024 * W + 512 * half
                        src_ready[0] = []
                        emit_ln(xT, t0, xn1, 0)
                        emit_ffn(xn1, t0, xT, x1T, t0)
                        src_ready[0] = [(sx[0], sx[0].n), (sx[1], sx[1].n)]
                        emit_ln(x1T, t0, xn2, 512 * half)
                    t0 = 1024 * W
                    own = t0 < TOWN
                    xn2_ready = vm.n
                    p.w("t", vm, xn2_ready)
                    cts = list(range(32, 96)) + (list(range(0, 32)) if own else [])
                    for ci, ct in enumerate(cts):
                        pa = ci % 2
                        pb = ci % 2
                        p.w("s", tm, wsl_consumed[pa])
                        p.do("s", lambda e, ct=ct, pa=pa:
                             e.dma_start(out=wsl[pa][:, :, :],
                                         in_=wqkv_bf.rearrange("(dc p) c -> p dc c", p=128)[:, :, 128 * ct:128 * (ct + 1)]),
                             inc=ws[pa], k=16)
                        p.w("t", ws[pa], ws[pa].n)
                        p.w("t", am, qk_banks_free[pb])
                        for dc in range(NDC):
                            for hh in range(2):
                                p.do("t", lambda e, dc=dc, hh=hh, pa=pa, pb=pb:
                                     e.matmul(bank(2 * pb + hh), wsl[pa][:, dc, :],
                                              xn2[:, dc, 512 * hh:512 * (hh + 1)],
                                              start=(dc == 0), stop=(dc == NDC - 1)),
                                     inc=(tm if (dc == NDC - 1 and hh == 1) else None))
                        wsl_consumed[pa] = tm.n
                        p.w("a", tm, tm.n)
                        p.w("a", sk[pb], kst_stage_free[pb])
                        p.do("a", lambda e, ct=ct, pb=pb:
                             e.activation(out=kst[pb][:, :], in_=bank(2 * pb, 2), func=AF.Identity,
                                          bias=qkb_sb[:, ct:ct + 1]), inc=am)
                        qk_banks_free[pb] = am.n
                        p.w("a", am, am.n)
                        if ct >= 64:
                            h, tgt = ct - 64, vbufT
                        elif ct >= 32:
                            h, tgt = ct - 32, kbuf
                        else:
                            h, tgt = ct, qbuf
                        p.do("a", lambda e, h=h, t0=t0, pb=pb, tgt=tgt:
                             e.dma_start(out=tgt[h, :, t0:t0 + 1024], in_=kst[pb][:, :]), inc=sk[pb], k=16)
                        kst_stage_free[pb] = sk[pb].n
                    p.w("a", sk[0], sk[0].n)
                    p.w("a", sk[1], sk[1].n)
                    scoresT_free[0] = am.n
                    p.barrier(bar)
            qkv_stored = (sk[0].n, sk[1].n)

        if "x1" in dump:
            p.do("s", lambda e: e.dma_start(out=dumps["d_x1"], in_=x1T), inc=dmp, k=16)
        if "kvq" in dump:
            p.do("s", lambda e: e.dma_start(out=dumps["d_k"], in_=kbuf), inc=dmp, k=16)
            p.do("s", lambda e: e.dma_start(out=dumps["d_v"], in_=vbufT), inc=dmp, k=16)
            p.do("s", lambda e: e.dma_start(out=dumps["d_q"], in_=qbuf), inc=dmp, k=16)
        if dump:
            p.w("s", dmp, dmp.n)
            p.barrier(bar)

        with ExitStack() as p34:
            ctx_all = p34.enter_context(nc.sbuf_tensor("ctx_all", [128, NH, TOWN], BF16))
            p3 = ExitStack()
            kst3_s = p3.enter_context(nc.sbuf_tensor("kst3", [128, TALL], BF16))
            kst3 = [kst3_s, kst3_s]
            vst3 = [p3.enter_context(nc.sbuf_tensor(f"vst3{i}", [128, TALL], BF16)) for i in range(2)]
            vtT = p3.enter_context(nc.sbuf_tensor("vtT", [128, TALL], BF16))
            qst3 = p3.enter_context(nc.sbuf_tensor("qst3", [128, TOWN], BF16))
            pt = p3.enter_context(nc.sbuf_tensor("pt", [128, 4, 1024], BF16))
            zacc = p3.enter_context(nc.sbuf_tensor("zacc", [128, 1024], F32))
            zb_sb = p3.enter_context(nc.sbuf_tensor("zb_sb", [128, 1024], F32))
            zrow = p3.enter_context(nc.sbuf_tensor("zrow", [1, 1024], F32))
            zrec = p3.enter_context(nc.sbuf_tensor("zrec", [1, 1024], F32))

            kv_consumed = [0, 0]
            k_consumed = [0]
            vt_evac = [0, 0]
            q_consumed = 0
            pt_slots = [(0, 0)] * 4
            st_free = [0, 0]
            ctx_free = 0
            z_banks_free = 0

            vt_done = [0]
            zb_evac = [0]
            p.w("s", sk[0], qkv_stored[0])
            p.w("s", sk[1], qkv_stored[1])
            for h in range(NH):
                pa = h % 2
                p.w("s", tm, k_consumed[0])
                p.do("s", lambda e, h=h, pa=pa: e.dma_start(out=kst3[pa][:, :], in_=kbuf[h, :, :]), inc=k3s, k=16)
                p.w("s", tm, kv_consumed[pa])
                p.w("s", am, vt_done[0])
                p.do("s", lambda e, h=h: e.dma_start(out=vtT[:, :], in_=vbufT[h, :, :]), inc=v3s, k=16)
                p.w("s", tm, q_consumed)
                p.do("s", lambda e, h=h: e.dma_start(out=qst3[:, :], in_=qbuf[h, :, :]), inc=q3s, k=16)
                p.w("t", k3s, k3s.n)
                p.w("t", v3s, v3s.n)
                p.w("t", q3s, q3s.n)
                p.w("t", vm, z_banks_free)
                p.w("t", am, zb_evac[0])
                for g8 in range(8):
                    zbk = 6 + (g8 % 2)
                    if g8 >= 2:
                        p.w("t", am, vt_evac[g8 % 2])
                    for ti in range(4):
                        kt = 4 * g8 + ti
                        p.do("t", lambda e, kt=kt, ti=ti, zbk=zbk:
                             e.matmul(bank(zbk)[:, 128 * ti:128 * (ti + 1)], vtT[:, 128 * kt:128 * (kt + 1)],
                                      id_bf[:, :], start=True, stop=True),
                             inc=(tm if ti == 3 else None))
                    p.w("a", tm, tm.n)
                    p.do("a", lambda e, g8=g8, pa=pa, zbk=zbk:
                         e.activation(out=vst3[pa][:, 512 * g8:512 * (g8 + 1)], in_=bank(zbk), func=AF.Identity),
                         inc=am)
                    vt_evac[g8 % 2] = am.n
                vt_done[0] = am.n
                for qc in range(2):
                    for kt in range(32):
                        stp = kt % 2
                        slot = kt % 4
                        p.w("t", am, st_free[stp])
                        for hh in range(2):
                            p.do("t", lambda e, kt=kt, qc=qc, hh=hh, stp=stp, pa=pa:
                                 e.matmul(bank(2 * stp + hh), kst3[pa][:, 128 * kt:128 * (kt + 1)],
                                          qst3[:, 1024 * qc + 512 * hh:1024 * qc + 512 * (hh + 1)],
                                          start=True, stop=True),
                                 inc=(tm if hh == 1 else None))
                        st_mm = tm.n
                        if qc == 1 and kt == 31:
                            q_consumed = tm.n
                            k_consumed[0] = tm.n
                        p.w("a", tm, st_mm)
                        p.w("a", vm, pt_slots[slot][0])
                        p.w("a", tm, pt_slots[slot][1])
                        p.do("a", lambda e, slot=slot, stp=stp:
                             e.activation(out=pt[:, slot, :], in_=bank(2 * stp, 2), func=AF.Exp,
                                          scale=ISQ, bias=cexp[:, :]), inc=am)
                        st_free[stp] = am.n
                        p.w("v", am, am.n)
                        if kt == 0:
                            p.w("v", vm, ctx_free)
                            p.do("v", lambda e, slot=slot: e.tensor_copy(out=zacc[:, :], in_=pt[:, slot, :]), inc=vm)
                        else:
                            p.w("v", vm, vm.n)
                            p.do("v", lambda e, slot=slot: e.tensor_tensor(out=zacc[:, :], in0=zacc[:, :],
                                                                           in1=pt[:, slot, :], op=ALU.add), inc=vm)
                        zadd = vm.n
                        p.w("t", am, am.n)
                        if kt == 0:
                            p.w("t", vm, ctx_free)
                            p.w("t", am, vt_done[0])
                        for hh in range(2):
                            p.do("t", lambda e, kt=kt, slot=slot, hh=hh, pa=pa:
                                 e.matmul(bank(4 + hh), vst3[pa][:, 128 * kt:128 * (kt + 1)],
                                          pt[:, slot, 512 * hh:512 * (hh + 1)],
                                          start=(kt == 0), stop=(kt == 31)),
                                 inc=(tm if hh == 1 else None))
                        pt_slots[slot] = (zadd, tm.n)
                        if qc == 1 and kt == 31:
                            kv_consumed[pa] = tm.n
                    pv_done = tm.n
                    p.w("t", vm, vm.n)
                    p.w("t", vm, z_banks_free)
                    p.do("t", lambda e: e.matmul(bank(6)[0:1, :], ones_f[:, :], zacc[:, 0:512], start=True, stop=True))
                    p.do("t", lambda e: e.matmul(bank(7)[0:1, :], ones_f[:, :], zacc[:, 512:1024], start=True, stop=True), inc=tm)
                    p.w("v", tm, tm.n)
                    p.do("v", lambda e: e.tensor_copy(out=zrow[0:1, 0:512], in_=bank(6)[0:1, :]))
                    p.do("v", lambda e: e.tensor_copy(out=zrow[0:1, 512:1024], in_=bank(7)[0:1, :]), inc=vm)
                    p.w("v", vm, vm.n)
                    p.do("v", lambda e: e.reciprocal(out=zrec[0:1, :], in_=zrow[0:1, :]), inc=vm)
                    p.w("t", vm, vm.n)
                    p.do("t", lambda e: e.matmul(bank(6), ones1[0:1, :], zrec[0:1, 0:512], start=True, stop=True))
                    p.do("t", lambda e: e.matmul(bank(7), ones1[0:1, :], zrec[0:1, 512:1024], start=True, stop=True), inc=tm)
                    p.w("a", tm, tm.n)
                    p.do("a", lambda e: e.activation(out=zb_sb[:, :], in_=bank(6, 2), func=AF.Identity), inc=am)
                    zb_evac[0] = am.n
                    p.w("v", am, am.n)
                    p.w("v", tm, pv_done)
                    p.do("v", lambda e, h=h, qc=qc: e.tensor_tensor(out=ctx_all[:, h, 1024 * qc:1024 * (qc + 1)],
                                                                    in0=bank(4, 2), in1=zb_sb[:, :], op=ALU.mult), inc=vm)
                    ctx_free = vm.n
                    z_banks_free = vm.n

            p.barrier(bar)
            p3.close()

            with ExitStack() as p4:
                owsl = [p4.enter_context(nc.sbuf_tensor(f"owsl{i}", [128, NDC, 128], BF16)) for i in range(2)]
                x1t = [p4.enter_context(nc.sbuf_tensor(f"x1t{i}", [128, TOWN], F32)) for i in range(2)]
                otmp = [p4.enter_context(nc.sbuf_tensor(f"otmp{i}", [128, TOWN], F32)) for i in range(2)]

                ow_consumed = [0, 0]
                o_banks_free = [0, 0]
                o_res = [0, 0]
                o_st = [0, 0]
                ctx_ready = vm.n
                p.w("s", sx[0], sx[0].n)
                p.w("s", sx[1], sx[1].n)
                for ct in range(32):
                    pa = ct % 2
                    p.w("s", tm, ow_consumed[pa])
                    p.do("s", lambda e, ct=ct, pa=pa:
                         e.dma_start(out=owsl[pa][:, :, :],
                                     in_=ow_bf.rearrange("(dc p) c -> p dc c", p=128)[:, :, 128 * ct:128 * (ct + 1)]),
                         inc=wo[pa], k=16)
                    p.w("s", vm, o_res[pa])
                    p.do("s", lambda e, ct=ct, pa=pa:
                         e.dma_start(out=x1t[pa][:, :], in_=x1T[128 * ct:128 * (ct + 1), 0:TOWN]), inc=xl[pa], k=16)
                    p.w("t", wo[pa], wo[pa].n)
                    p.w("t", vm, ctx_ready)
                    p.w("t", am, o_banks_free[pa])
                    for hh in range(NH):
                        for q4 in range(4):
                            p.do("t", lambda e, hh=hh, q4=q4, pa=pa:
                                 e.matmul(bank(4 * pa + q4), owsl[pa][:, hh, :], ctx_all[:, hh, 512 * q4:512 * (q4 + 1)],
                                          start=(hh == 0), stop=(hh == NH - 1)),
                                 inc=(tm if (hh == NH - 1 and q4 == 3) else None))
                    ow_consumed[pa] = tm.n
                    p.w("a", tm, tm.n)
                    p.w("a", vm, o_res[pa])
                    p.do("a", lambda e, ct=ct, pa=pa: e.activation(out=otmp[pa][:, :], in_=bank(4 * pa, 4),
                                                                   func=AF.Identity, bias=ob_sb[:, ct:ct + 1]), inc=am)
                    o_banks_free[pa] = am.n
                    p.w("v", am, am.n)
                    p.w("v", xl[pa], xl[pa].n)
                    p.w("v", s2[pa], o_st[pa])
                    p.do("v", lambda e, pa=pa: e.tensor_tensor(out=otmp[pa][:, :], in0=otmp[pa][:, :],
                                                               in1=x1t[pa][:, :], op=ALU.add), inc=vm)
                    o_res[pa] = vm.n
                    p.w("a", vm, vm.n)
                    p.do("a", lambda e, ct=ct, pa=pa:
                         e.dma_start(out=x2T[128 * ct:128 * (ct + 1), :], in_=otmp[pa][:, :]), inc=s2[pa], k=16)
                    o_st[pa] = s2[pa].n
                p.w("a", s2[0], s2[0].n)
                p.w("a", s2[1], s2[1].n)
                p.barrier(bar)
            x2_stored = (s2[0].n, s2[1].n)

        if "x2" in dump:
            p.do("s", lambda e: e.dma_start(out=dumps["d_x2"], in_=x2T), inc=dmp, k=16)
            p.w("s", dmp, dmp.n)
            p.barrier(bar)

        with ExitStack() as p5:
            buf["xf"] = p5.enter_context(nc.sbuf_tensor("xf5", [128, NDC, 128], F32))
            buf["xs2"] = p5.enter_context(nc.sbuf_tensor("xs25", [128, NDC, 128], F32))
            buf["xb2"] = p5.enter_context(nc.sbuf_tensor("xb25", [128, NDC, 128], BF16))
            buf["xq2"] = p5.enter_context(nc.sbuf_tensor("xq25", [128, NDC, 128], BF16))
            buf["hg"] = p5.enter_context(nc.sbuf_tensor("hg5", [128, 8, 512], BF16))
            buf["fslab"] = [p5.enter_context(nc.sbuf_tensor(f"fslab5{i}", [128, 8192], BF16)) for i in range(2)]
            buf["xres"] = [p5.enter_context(nc.sbuf_tensor(f"xres5{i}", [128, 512], F32)) for i in range(2)]
            buf["xout"] = [p5.enter_context(nc.sbuf_tensor(f"xout5{i}", [128, 512], F32)) for i in range(2)]
            buf["snat"] = p5.enter_context(nc.sbuf_tensor("snat5", [128, 4, 64], F32))
            buf["sT_sb"] = p5.enter_context(nc.sbuf_tensor("sT_sb5", [64, 512], F32))
            buf["GT_sb"] = p5.enter_context(nc.sbuf_tensor("GT_sb5", [64, 512], F32))
            buf["gx_sb"] = p5.enter_context(nc.sbuf_tensor("gx_sb5", [128, 512], F32))
            xn5 = p5.enter_context(nc.sbuf_tensor("xn5", [128, NDC, 512], BF16))

            src_ready[0] = [(s2[0], x2_stored[0]), (s2[1], x2_stored[1])]
            p.w("s", s2[0], x2_stored[0])
            p.w("s", s2[1], x2_stored[1])
            for w5 in range(4):
                t0 = 512 * w5
                emit_ln(x2T, t0, xn5, 0)
                emit_ffn(xn5, t0, x2T, outT, t0)

        p.w("s", sx[0], sx[0].n)
        p.w("s", sx[1], sx[1].n)
        p.barrier(bar)
        p.flush()
    return nc, dumps


_NC_CACHE = {}


def _get_nc(dump=()):
    key = tuple(sorted(dump))
    if key not in _NC_CACHE:
        _NC_CACHE[key] = build(dump)
    return _NC_CACHE[key]


def _host_inputs(L0_attn_x, L0_attn_attn_qkvw, L0_attn_attn_qkvb, L0_attn_attn_ow,
                 L0_attn_attn_ob, L0_ffn_A, L0_ffn_B, L0_ffn_Wr):
    x = np.asarray(L0_attn_x, np.float32)
    wq = np.ascontiguousarray(np.asarray(L0_attn_attn_qkvw, np.float32))
    qkvb = np.asarray(L0_attn_attn_qkvb, np.float32)
    owh = np.ascontiguousarray(np.asarray(L0_attn_attn_ow, np.float32))
    obh = np.asarray(L0_attn_attn_ob, np.float32)
    A = np.asarray(L0_ffn_A, np.float32)
    B = np.asarray(L0_ffn_B, np.float32)
    Wr = np.asarray(L0_ffn_Wr, np.float32)

    wrT = np.ascontiguousarray(Wr.T)
    aTh = np.ascontiguousarray(np.transpose(A, (2, 0, 1)).reshape(D, 1024))
    bfh = np.ascontiguousarray(np.transpose(B, (0, 2, 1)).reshape(1024, D))
    qkbh = np.ascontiguousarray(qkvb.reshape(96, 128).T)
    obrh = np.ascontiguousarray(obh.reshape(32, 128).T)
    em = np.zeros((64, 1024), np.float32)
    em[np.arange(1024) >> 4, np.arange(1024)] = 1.0
    idh = np.eye(128, dtype=np.float32)

    shared = dict(wqkv=wq, ow=owh, wrT=wrT, aT=aTh, bfm=bfh, qkb=qkbh,
                  obr=obrh, emat=em, idf=idh)
    in_maps = []
    for c in range(8):
        b, hf = c // 2, c % 2
        own = x[b, 2048 * hf:2048 * (hf + 1), :]
        sib = x[b, 2048 * (1 - hf):2048 * (2 - hf), :]
        xTh = np.ascontiguousarray(np.concatenate([own, sib], 0).T)
        in_maps.append(dict(shared, xT=xTh))
    return in_maps


def kernel(L0_attn_x, L0_attn_attn_qkvw, L0_attn_attn_qkvb, L0_attn_attn_ow,
           L0_attn_attn_ob, L0_ffn_A, L0_ffn_B, L0_ffn_Wr):
    in_maps = _host_inputs(L0_attn_x, L0_attn_attn_qkvw, L0_attn_attn_qkvb,
                           L0_attn_attn_ow, L0_attn_attn_ob, L0_ffn_A, L0_ffn_B, L0_ffn_Wr)
    nc, _ = _get_nc()
    res = run_bass_kernel_spmd(nc, in_maps, core_ids=list(range(8)))
    out = np.empty((4, 4096, 4096), np.float32)
    for c in range(8):
        b, hf = c // 2, c % 2
        out[b, 2048 * hf:2048 * (hf + 1), :] = res.results[c]["outT"].T
    return out


# version 08
# speedup vs baseline: 1.0173x; 1.0173x over previous
import numpy as np
import ml_dtypes
from contextlib import ExitStack
import concourse.bass as bass
import concourse.mybir as mybir
from concourse.bass_utils import run_bass_kernel_spmd

F32 = mybir.dt.float32
BF16 = mybir.dt.bfloat16
AF = mybir.ActivationFunctionType
ALU = mybir.AluOpType
AX = mybir.AxisListType
bfdt = ml_dtypes.bfloat16

D = 4096
NDC = 32
TALL = 4096
TOWN = 2048
NH = 32
HD = 128
EPS = 1e-5
ISQ = float(1.0 / np.sqrt(128.0))
EXPC = 4.0


class S:
    def __init__(self, h):
        self.h = h
        self.n = 0


class P:
    def __init__(self, nc):
        self.nc = nc
        self.ops = {k: [] for k in "tvags"}
        self.nbar = 0

    def w(self, eng, sem, val):
        if val > 0:
            self.ops[eng].append(lambda e, h=sem.h, v=val: e.wait_ge(h, v))

    def do(self, eng, fn, inc=None, k=1):
        if inc is None:
            self.ops[eng].append(fn)
            return None
        inc.n += k
        self.ops[eng].append(lambda e, fn=fn, h=inc.h, k=k: fn(e).then_inc(h, k))
        return inc.n

    def barrier(self, bar):
        self.nbar += 1
        tgt = 5 * self.nbar
        for eng in "tvags":
            bar.n += 1
            if eng == "s":
                self.ops[eng].append(lambda e, h=bar.h: e.sem_inc(h, 1))
            else:
                self.ops[eng].append(lambda e, h=bar.h: e.drain().then_inc(h, 1))
            self.ops[eng].append(lambda e, h=bar.h, t=tgt: e.wait_ge(h, t))

    def flush(self):
        with self.nc.Block() as blk:
            blk.tensor(lambda e: [f(e) for f in self.ops["t"]])
            blk.vector(lambda e: [f(e) for f in self.ops["v"]])
            blk.scalar(lambda e: [f(e) for f in self.ops["a"]])
            blk.gpsimd(lambda e: [f(e) for f in self.ops["g"]])
            blk.sync(lambda e: [f(e) for f in self.ops["s"]])


def rearr_dT(ap, t0, tw):
    return ap.rearrange("(dc p) t -> p dc t", p=128)[:, :, t0:t0 + tw]


def bcast_dc(ap2d, ndc=NDC):
    a = ap2d
    return bass.AP(a.tensor, a.offset, [list(a.ap[0]), [0, ndc], list(a.ap[1])])


def build(dump=(), paranoid=False, skip_attn=False):
    nc = bass.Bass("TRN2", target_bir_lowering=False)

    xT = nc.dram_tensor("xT", [D, TALL], F32, kind="ExternalInput").ap()
    wqkv = nc.dram_tensor("wqkv", [D, 3 * D], F32, kind="ExternalInput").ap()
    ow = nc.dram_tensor("ow", [D, D], F32, kind="ExternalInput").ap()
    wrT = nc.dram_tensor("wrT", [D, 64], F32, kind="ExternalInput").ap()
    aT = nc.dram_tensor("aT", [D, 1024], F32, kind="ExternalInput").ap()
    bfm = nc.dram_tensor("bfm", [1024, D], F32, kind="ExternalInput").ap()
    qkb = nc.dram_tensor("qkb", [128, 96], F32, kind="ExternalInput").ap()
    obr = nc.dram_tensor("obr", [128, 32], F32, kind="ExternalInput").ap()
    emat = nc.dram_tensor("emat", [64, 1024], F32, kind="ExternalInput").ap()
    idf = nc.dram_tensor("idf", [128, 128], F32, kind="ExternalInput").ap()

    outT = nc.dram_tensor("outT", [D, TOWN], F32, kind="ExternalOutput").ap()

    wqkv_bf = nc.dram_tensor("wqkv_bf", [D, 3 * D], BF16, kind="Internal").ap()
    ow_bf = nc.dram_tensor("ow_bf", [D, D], BF16, kind="Internal").ap()
    aT_bf = nc.dram_tensor("aT_bf", [D, 1024], BF16, kind="Internal").ap()
    bf_bf = nc.dram_tensor("bf_bf", [1024, D], BF16, kind="Internal").ap()
    x1T = nc.dram_tensor("x1T", [D, TALL], F32, kind="Internal").ap()
    x2T = nc.dram_tensor("x2T", [D, TOWN], F32, kind="Internal").ap()
    kbuf = nc.dram_tensor("kbuf", [NH, HD, TALL], BF16, kind="Internal").ap()
    vbufT = nc.dram_tensor("vbufT", [NH, HD, TALL], BF16, kind="Internal").ap()
    qbuf = nc.dram_tensor("qbuf", [NH, HD, TOWN], BF16, kind="Internal").ap()

    dumps = {}
    if "x1" in dump:
        dumps["d_x1"] = nc.dram_tensor("d_x1", [D, TALL], F32, kind="ExternalOutput").ap()
    if "kvq" in dump:
        dumps["d_k"] = nc.dram_tensor("d_k", [NH, HD, TALL], BF16, kind="ExternalOutput").ap()
        dumps["d_v"] = nc.dram_tensor("d_v", [NH, HD, TALL], BF16, kind="ExternalOutput").ap()
        dumps["d_q"] = nc.dram_tensor("d_q", [NH, HD, TOWN], BF16, kind="ExternalOutput").ap()
    if "x2" in dump:
        dumps["d_x2"] = nc.dram_tensor("d_x2", [D, TOWN], F32, kind="ExternalOutput").ap()

    es = ExitStack()
    with es:
        ones_bf = es.enter_context(nc.sbuf_tensor("ones_bf", [128, 1], BF16))
        ones_f = es.enter_context(nc.sbuf_tensor("ones_f", [128, 1], F32))
        ones1 = es.enter_context(nc.sbuf_tensor("ones1", [1, 128], F32))
        wr_sb = es.enter_context(nc.sbuf_tensor("wr_sb", [128, NDC, 64], BF16))
        e_sb = es.enter_context(nc.sbuf_tensor("e_sb", [64, 1024], F32))
        id_sb = es.enter_context(nc.sbuf_tensor("id_sb", [128, 128], F32))
        qkb_sb = es.enter_context(nc.sbuf_tensor("qkb_sb", [128, 96], F32))
        id_bf = es.enter_context(nc.sbuf_tensor("id_bf", [128, 128], BF16))
        ob_sb = es.enter_context(nc.sbuf_tensor("ob_sb", [128, 32], F32))
        rows_holder = {}
        epsb = es.enter_context(nc.sbuf_tensor("epsb", [1, 1], F32))
        cexp = es.enter_context(nc.sbuf_tensor("cexp", [128, 1], F32))
        m8a = es.enter_context(nc.sbuf_tensor("m8a", [128, 8], F32))
        m8b = es.enter_context(nc.sbuf_tensor("m8b", [128, 8], F32))
        ntau = es.enter_context(nc.sbuf_tensor("ntau", [128, 1], F32))
        mrs = es.enter_context(nc.sbuf_tensor("mrs", [128, 64], F32))
        etile = es.enter_context(nc.sbuf_tensor("etile", [128, 64], F32))
        gsum = es.enter_context(nc.sbuf_tensor("gsum", [128, 1], F32))
        grec = es.enter_context(nc.sbuf_tensor("grec", [128, 1], F32))
        gtile = es.enter_context(nc.sbuf_tensor("gtile", [128, 64], F32))

        PS = es.enter_context(nc.psum_tensor("PS", [128, 4096], F32))

        def bank(i, n=1):
            return PS[:, 512 * i:512 * (i + n)]

        tm = S(es.enter_context(nc.semaphore("tm")))
        vm = S(es.enter_context(nc.semaphore("vm")))
        am = S(es.enter_context(nc.semaphore("am")))
        bar = S(es.enter_context(nc.semaphore("bar")))
        cst = S(es.enter_context(nc.semaphore("cst")))
        lnx = [S(es.enter_context(nc.semaphore(f"lnx{i}"))) for i in range(2)]
        dmp = S(es.enter_context(nc.semaphore("dmp")))
        c_ = [S(es.enter_context(nc.semaphore(f"c{i}"))) for i in range(2)]
        sc = [S(es.enter_context(nc.semaphore(f"sc{i}"))) for i in range(2)]
        fs = [S(es.enter_context(nc.semaphore(f"fs{i}"))) for i in range(2)]
        xr = [S(es.enter_context(nc.semaphore(f"xr{i}"))) for i in range(2)]
        sx = [S(es.enter_context(nc.semaphore(f"sx{i}"))) for i in range(2)]
        ws = [S(es.enter_context(nc.semaphore(f"ws{i}"))) for i in range(2)]
        sk = [S(es.enter_context(nc.semaphore(f"sk{i}"))) for i in range(2)]
        k3s = S(es.enter_context(nc.semaphore("k3s")))
        v3s = S(es.enter_context(nc.semaphore("v3s")))
        q3s = S(es.enter_context(nc.semaphore("q3s")))
        wo = [S(es.enter_context(nc.semaphore(f"wo{i}"))) for i in range(2)]
        xl = [S(es.enter_context(nc.semaphore(f"xl{i}"))) for i in range(2)]
        s2 = [S(es.enter_context(nc.semaphore(f"s2{i}"))) for i in range(2)]

        p = P(nc)

        ln_free = [0, 0]
        ln_q_free = [0]
        ln_banks_free = [0]
        src_ready = [[]]
        scoresT_free = [0]
        slab_consumed = [0, 0]
        gexp_evac = [0]
        gx_read = [0]
        gT_read = [0]
        hg_free = [0, 0]
        res_done = [0, 0]
        out_stored = [0, 0]
        buf = {}

        def emit_ln(src, t0, xnbf, off):
            for (s_, v_) in src_ready[0]:
                p.w("s", s_, v_)
            for sci in range(4):
                lp = sci % 2
                xf, xb2, xq2 = buf["xf"][lp], buf["xb2"][lp], buf["xq2"]
                tt0 = t0 + 128 * sci
                p.w("s", vm, ln_free[lp])
                p.do("s", lambda e, src=src, tt0=tt0, xf=xf:
                     e.dma_start(out=xf[:, :, :], in_=rearr_dT(src, tt0, 128)), inc=lnx[lp], k=16)
                p.w("v", lnx[lp], lnx[lp].n)
                p.do("v", lambda e, xf=xf, xb2=xb2: e.tensor_copy(out=xb2[:, :, :], in_=xf[:, :, :]), inc=vm)
                p.w("v", vm, vm.n)
                p.w("v", tm, ln_q_free[0])
                p.do("v", lambda e, xb2=xb2, xq2=xq2:
                     e.tensor_mul(out=xq2[:, :, :], in0=xb2[:, :, :], in1=xb2[:, :, :]), inc=vm)
                p.w("t", vm, vm.n)
                p.w("t", vm, ln_banks_free[0])
                for dc in range(NDC):
                    p.do("t", lambda e, dc=dc, xb2=xb2:
                         e.matmul(bank(6)[0:1, 0:128], ones_bf[:, :], xb2[:, dc, :],
                                  start=(dc == 0), stop=(dc == NDC - 1)))
                for dc in range(NDC):
                    p.do("t", lambda e, dc=dc, xq2=xq2:
                         e.matmul(bank(7)[0:1, 0:128], ones_bf[:, :], xq2[:, dc, :],
                                  start=(dc == 0), stop=(dc == NDC - 1)),
                         inc=(tm if dc == NDC - 1 else None))
                ln_q_free[0] = tm.n
                p.w("v", tm, tm.n)
                p.do("v", lambda e: e.tensor_scalar_mul(out=buf['rows'][0:1, 0, :], in0=bank(6)[0:1, 0:128], scalar1=1.0 / D))
                p.do("v", lambda e: e.tensor_scalar_mul(out=buf['rows'][0:1, 1, :], in0=bank(7)[0:1, 0:128], scalar1=1.0 / D), inc=vm)
                p.w("v", vm, vm.n)
                p.do("v", lambda e: e.tensor_mul(out=buf['rows'][0:1, 2, :], in0=buf['rows'][0:1, 0, :], in1=buf['rows'][0:1, 0, :]), inc=vm)
                p.w("v", vm, vm.n)
                p.do("v", lambda e: e.tensor_sub(out=buf['rows'][0:1, 3, :], in0=buf['rows'][0:1, 1, :], in1=buf['rows'][0:1, 2, :]), inc=vm)
                p.w("a", vm, vm.n)
                p.do("a", lambda e: e.activation(out=buf['rows'][0:1, 4, :], in_=buf['rows'][0:1, 3, :], func=AF.Ln, bias=epsb[0:1, :]), inc=am)
                p.w("a", am, am.n)
                p.do("a", lambda e: e.activation(out=buf['rows'][0:1, 5, :], in_=buf['rows'][0:1, 4, :], func=AF.Exp, scale=-0.5), inc=am)
                p.w("t", am, am.n)
                p.do("t", lambda e: e.matmul(bank(6)[:, 0:128], ones1[0:1, :], buf['rows'][0:1, 0, :], start=True, stop=True))
                p.do("t", lambda e: e.matmul(bank(7)[:, 0:128], ones1[0:1, :], buf['rows'][0:1, 5, :], start=True, stop=True), inc=tm)
                p.w("v", tm, tm.n)
                p.do("v", lambda e, xf=xf:
                     e.tensor_tensor(out=xf[:, :, :], in0=xf[:, :, :],
                                     in1=bcast_dc(bank(6)[:, 0:128]), op=ALU.subtract), inc=vm)
                p.w("v", vm, vm.n)
                p.do("v", lambda e, xnbf=xnbf, off=off, sc=sci, xf=xf:
                     e.tensor_tensor(out=xnbf[:, :, off + 128 * sc:off + 128 * (sc + 1)],
                                     in0=xf[:, :, :], in1=bcast_dc(bank(7)[:, 0:128]),
                                     op=ALU.mult), inc=vm)
                ln_free[lp] = vm.n
                ln_banks_free[0] = vm.n

        def emit_ffn(xnbf, t0, resid_src, dst, dst_t0):
            hg, fslab, xres, xout = buf["hg"], buf["fslab"], buf["xres"], buf["xout"]
            p.w("t", vm, vm.n)
            p.w("t", am, scoresT_free[0])
            for dc in range(NDC):
                p.do("t", lambda e, dc=dc, xnbf=xnbf:
                     e.matmul(bank(3)[0:64, :], wr_sb[:, dc, :], xnbf[:, dc, :],
                              start=(dc == 0), stop=(dc == NDC - 1)),
                     inc=(tm if dc == NDC - 1 else None))
            p.w("a", tm, tm.n)
            p.do("a", lambda e: e.activation(out=buf['sT_sb'][:, :], in_=bank(3)[0:64, :], func=AF.Identity), inc=am)
            scoresT_free[0] = am.n
            p.w("t", am, am.n)
            for tt in range(4):
                p.do("t", lambda e, tt=tt: e.transpose(bank(4)[:, 64 * tt:64 * (tt + 1)],
                                                       buf['sT_sb'][:, 128 * tt:128 * (tt + 1)], id_sb[0:64, 0:64]),
                     inc=(tm if tt == 3 else None))
            p.w("a", tm, tm.n)
            p.do("a", lambda e: e.activation(out=buf['snat'][:, :, :], in_=bank(4)[:, 0:256], func=AF.Identity), inc=am)
            for tt in range(4):
                p.w("v", am, am.n)
                p.do("v", lambda e, tt=tt: e.max(out=m8a[:, :], in_=buf['snat'][:, tt, :]), inc=vm)
                p.w("v", vm, vm.n)
                p.do("v", lambda e, tt=tt: e.match_replace(out=mrs[:, :], in_to_replace=m8a[:, :],
                                                           in_values=buf['snat'][:, tt, :], imm_value=-1e30), inc=vm)
                p.w("v", vm, vm.n)
                p.do("v", lambda e: e.max(out=m8b[:, :], in_=mrs[:, :]), inc=vm)
                p.w("v", vm, vm.n)
                p.do("v", lambda e: e.tensor_scalar_mul(out=ntau[:, :], in0=m8b[:, 7:8], scalar1=-1.0), inc=vm)
                p.w("a", vm, vm.n)
                p.do("a", lambda e, tt=tt: e.activation(out=etile[:, :], in_=buf['snat'][:, tt, :], func=AF.Exp,
                                                        bias=ntau[:, :]), inc=am)
                p.w("v", am, am.n)
                p.do("v", lambda e, tt=tt: e.tensor_scalar(out=mrs[:, :], in0=buf['snat'][:, tt, :],
                                                           scalar1=m8b[:, 7:8], scalar2=None, op0=ALU.is_ge), inc=vm)
                p.w("v", vm, vm.n)
                p.do("v", lambda e: e.tensor_mul(out=etile[:, :], in0=etile[:, :], in1=mrs[:, :]), inc=vm)
                p.w("v", vm, vm.n)
                p.do("v", lambda e: e.reduce_sum(out=gsum[:, :], in_=etile[:, :], axis=AX.X), inc=vm)
                p.w("v", vm, vm.n)
                p.do("v", lambda e: e.reciprocal(out=grec[:, :], in_=gsum[:, :]), inc=vm)
                p.w("v", vm, vm.n)
                p.w("v", tm, gT_read[0])
                p.do("v", lambda e: e.tensor_scalar(out=gtile[:, :], in0=etile[:, :], scalar1=grec[:, :],
                                                    scalar2=None, op0=ALU.mult), inc=vm)
                p.w("t", vm, vm.n)
                p.do("t", lambda e, tt=tt: e.transpose(bank(5)[0:64, 128 * tt:128 * (tt + 1)],
                                                       gtile[:, :], id_sb[:, :]), inc=tm)
                gT_read[0] = tm.n
            p.w("a", tm, tm.n)
            p.do("a", lambda e: e.activation(out=buf['GT_sb'][:, :], in_=bank(5)[0:64, :], func=AF.Identity), inc=am)
            gt_ready = am.n

            for jp in range(4):
                pa = jp % 2
                p.w("s", tm, slab_consumed[pa])
                p.do("s", lambda e, jp=jp, pa=pa, fslab=fslab:
                     e.dma_start(out=fslab[pa][:, 0:NDC * 256].rearrange("p (dc c) -> p dc c", c=256),
                                 in_=aT_bf.rearrange("(dc p) j -> p dc j", p=128)[:, :, 256 * jp:256 * (jp + 1)]),
                     inc=fs[pa], k=16)
                p.w("t", fs[pa], fs[pa].n)
                p.w("t", am, gt_ready)
                for j01 in range(2):
                    j = 2 * jp + j01
                    p.w("t", vm, hg_free[j01])
                    for dc in range(NDC):
                        p.do("t", lambda e, dc=dc, j01=j01, pa=pa, fslab=fslab, xnbf=xnbf:
                             e.matmul(bank(j01),
                                      fslab[pa][:, 0:NDC * 256].rearrange("p (dc c) -> p dc c", c=256)[:, dc, 128 * j01:128 * (j01 + 1)],
                                      xnbf[:, dc, :], start=(dc == 0), stop=(dc == NDC - 1)))
                    p.w("t", am, gexp_evac[0])
                    p.do("t", lambda e, j=j: e.matmul(bank(2), e_sb[:, 128 * j:128 * (j + 1)], buf['GT_sb'][:, :],
                                                      start=True, stop=True), inc=tm)
                    slab_consumed[pa] = tm.n
                    p.w("a", tm, tm.n)
                    p.w("a", vm, gx_read[0])
                    p.do("a", lambda e: e.activation(out=buf['gx_sb'][:, :], in_=bank(2), func=AF.Identity), inc=am)
                    gexp_evac[0] = am.n
                    p.w("v", am, am.n)
                    p.do("v", lambda e, j=j, j01=j01, hg=hg:
                         e.tensor_tensor(out=hg[:, j, :], in0=bank(j01), in1=buf['gx_sb'][:, :], op=ALU.mult), inc=vm)
                    hg_free[j01] = vm.n
                    gx_read[0] = vm.n
            hg_all = vm.n

            for ctg in range(4):
                pa = ctg % 2
                p.w("s", tm, slab_consumed[pa])
                p.do("s", lambda e, ctg=ctg, pa=pa, fslab=fslab:
                     e.dma_start(out=fslab[pa][:, :].rearrange("p (j c) -> p j c", c=1024),
                                 in_=bf_bf.rearrange("(j p) c -> p j c", p=128)[:, :, 1024 * ctg:1024 * (ctg + 1)]),
                     inc=fs[pa], k=16)
                p.w("t", fs[pa], fs[pa].n)
                p.w("t", vm, hg_all)
                for ct8 in range(8):
                    ct = 8 * ctg + ct8
                    pb = ct8 % 2
                    p.w("t", vm, res_done[pb])
                    for j in range(8):
                        p.do("t", lambda e, j=j, ct8=ct8, pa=pa, pb=pb, fslab=fslab, hg=hg:
                             e.matmul(bank(pb),
                                      fslab[pa][:, :].rearrange("p (j c) -> p j c", c=1024)[:, j, 128 * ct8:128 * (ct8 + 1)],
                                      hg[:, j, :], start=(j == 0), stop=(j == 7)),
                             inc=(tm if j == 7 else None))
                    slab_consumed[pa] = tm.n
                    p.w("s", vm, res_done[pb])
                    p.do("s", lambda e, ct=ct, t0=t0, pb=pb, resid_src=resid_src, xres=xres:
                         e.dma_start(out=xres[pb][:, :], in_=resid_src[128 * ct:128 * (ct + 1), t0:t0 + 512]),
                         inc=xr[pb], k=16)
                    p.w("v", tm, tm.n)
                    p.w("v", xr[pb], xr[pb].n)
                    p.w("v", sx[pb], out_stored[pb])
                    p.do("v", lambda e, pb=pb, xres=xres, xout=xout:
                         e.tensor_tensor(out=xout[pb][:, :], in0=bank(pb), in1=xres[pb][:, :], op=ALU.add), inc=vm)
                    res_done[pb] = vm.n
                    p.w("a", vm, vm.n)
                    p.do("a", lambda e, ct=ct, dst=dst, dst_t0=dst_t0, pb=pb, xout=xout:
                         e.dma_start(out=dst[128 * ct:128 * (ct + 1), dst_t0:dst_t0 + 512], in_=xout[pb][:, :]),
                         inc=sx[pb], k=16)
                    out_stored[pb] = sx[pb].n
            p.w("a", sx[0], sx[0].n)
            p.w("a", sx[1], sx[1].n)
            if paranoid:
                p.barrier(bar)

        p.do("v", lambda e: e.memset(ones_bf[:, :], 1.0))
        p.do("v", lambda e: e.memset(ones_f[:, :], 1.0))
        p.do("v", lambda e: e.memset(ones1[:, :], 1.0))
        p.do("v", lambda e: e.memset(epsb[:, :], EPS))
        p.do("v", lambda e: e.memset(cexp[:, :], -EXPC), inc=vm)
        p.do("s", lambda e: e.dma_start(out=e_sb[:, :], in_=emat), inc=cst, k=16)
        p.do("s", lambda e: e.dma_start(out=id_sb[:, :], in_=idf), inc=cst, k=16)
        p.do("s", lambda e: e.dma_start(out=qkb_sb[:, :], in_=qkb), inc=cst, k=16)
        p.do("s", lambda e: e.dma_start(out=ob_sb[:, :], in_=obr), inc=cst, k=16)
        p.w("v", cst, cst.n)
        p.do("v", lambda e: e.tensor_copy(out=id_bf[:, :], in_=id_sb[:, :]), inc=vm)

        if True:
            with ExitStack() as p0:
                cf = [p0.enter_context(nc.sbuf_tensor(f"cf{i}", [128, 3 * D], F32)) for i in range(2)]
                cb = [p0.enter_context(nc.sbuf_tensor(f"cb{i}", [128, 3 * D], BF16)) for i in range(2)]
                conv = [(wqkv, wqkv_bf, 32, 3 * D), (ow, ow_bf, 32, D),
                        (aT, aT_bf, 32, 1024), (bfm, bf_bf, 8, D), (wrT, None, 32, 64)]
                last_cvt = [0, 0]
                last_st = [0, 0]
                for (src, dst, nsl, cols) in conv:
                    for i in range(nsl):
                        pa = i % 2
                        p.w("s", vm, last_cvt[pa])
                        p.do("s", lambda e, src=src, i=i, cols=cols, pa=pa:
                             e.dma_start(out=cf[pa][:, 0:cols], in_=src[128 * i:128 * (i + 1), :]), inc=c_[pa], k=16)
                        p.w("v", c_[pa], c_[pa].n)
                        p.w("v", sc[pa], last_st[pa])
                        if dst is None:
                            p.do("v", lambda e, i=i, pa=pa:
                                 e.tensor_copy(out=wr_sb[:, i, :], in_=cf[pa][:, 0:64]), inc=vm)
                            last_cvt[pa] = vm.n
                        else:
                            p.do("v", lambda e, pa=pa, cols=cols:
                                 e.tensor_copy(out=cb[pa][:, 0:cols], in_=cf[pa][:, 0:cols]), inc=vm)
                            last_cvt[pa] = vm.n
                            p.w("a", vm, vm.n)
                            p.do("a", lambda e, dst=dst, i=i, cols=cols, pa=pa:
                                 e.dma_start(out=dst[128 * i:128 * (i + 1), :], in_=cb[pa][:, 0:cols]),
                                 inc=sc[pa], k=16)
                            last_st[pa] = sc[pa].n
                p.w("a", sc[0], sc[0].n)
                p.w("a", sc[1], sc[1].n)
                conv_done = (sc[0].n, sc[1].n)
                p.barrier(bar)

            with ExitStack() as p1:
                buf["xf"] = [p1.enter_context(nc.sbuf_tensor(f"xf_{i}", [128, NDC, 128], F32)) for i in range(2)]
                buf["xb2"] = [p1.enter_context(nc.sbuf_tensor(f"xb2_{i}", [128, NDC, 128], BF16)) for i in range(2)]
                buf["xq2"] = p1.enter_context(nc.sbuf_tensor("xq2", [128, NDC, 128], BF16))
                buf["rows"] = p1.enter_context(nc.sbuf_tensor("rows", [1, 6, 128], F32))
                buf["hg"] = p1.enter_context(nc.sbuf_tensor("hg", [128, 8, 512], BF16))
                buf["fslab"] = [p1.enter_context(nc.sbuf_tensor(f"fslab{i}", [128, 8192], BF16)) for i in range(2)]
                buf["xres"] = [p1.enter_context(nc.sbuf_tensor(f"xres{i}", [128, 512], F32)) for i in range(2)]
                buf["xout"] = [p1.enter_context(nc.sbuf_tensor(f"xout{i}", [128, 512], F32)) for i in range(2)]
                buf["snat"] = p1.enter_context(nc.sbuf_tensor("snat", [128, 4, 64], F32))
                buf["sT_sb"] = p1.enter_context(nc.sbuf_tensor("sT_sb", [64, 512], F32))
                buf["GT_sb"] = p1.enter_context(nc.sbuf_tensor("GT_sb", [64, 512], F32))
                buf["gx_sb"] = p1.enter_context(nc.sbuf_tensor("gx_sb", [128, 512], BF16))
                xn2 = p1.enter_context(nc.sbuf_tensor("xn2", [128, NDC, 1024], BF16))
                xn1 = xn2[:, :, 512:1024]
                wsl = [p1.enter_context(nc.sbuf_tensor(f"wsl{i}", [128, NDC, 128], BF16)) for i in range(2)]
                kst = [p1.enter_context(nc.sbuf_tensor(f"kst{i}", [128, 1024], BF16)) for i in range(2)]

                wsl_consumed = [0, 0]
                qk_banks_free = [0, 0]
                kst_stage_free = [0, 0]

                p.w("s", sc[0], conv_done[0])
                p.w("s", sc[1], conv_done[1])
                p.w("s", cst, cst.n)
                for W in range(4):
                    for half in range(2):
                        t0 = 1024 * W + 512 * half
                        src_ready[0] = []
                        emit_ln(xT, t0, xn1, 0)
                        emit_ffn(xn1, t0, xT, x1T, t0)
                        src_ready[0] = [(sx[0], sx[0].n), (sx[1], sx[1].n)]
                        emit_ln(x1T, t0, xn2, 512 * half)
                    t0 = 1024 * W
                    own = t0 < TOWN
                    xn2_ready = vm.n
                    p.w("t", vm, xn2_ready)
                    cts = list(range(32, 96)) + (list(range(0, 32)) if own else [])
                    for ci, ct in enumerate(cts):
                        pa = ci % 2
                        pb = ci % 2
                        p.w("s", tm, wsl_consumed[pa])
                        p.do("s", lambda e, ct=ct, pa=pa:
                             e.dma_start(out=wsl[pa][:, :, :],
                                         in_=wqkv_bf.rearrange("(dc p) c -> p dc c", p=128)[:, :, 128 * ct:128 * (ct + 1)]),
                             inc=ws[pa], k=16)
                        p.w("t", ws[pa], ws[pa].n)
                        p.w("t", am, qk_banks_free[pb])
                        for dc in range(NDC):
                            for hh in range(2):
                                p.do("t", lambda e, dc=dc, hh=hh, pa=pa, pb=pb:
                                     e.matmul(bank(2 * pb + hh), wsl[pa][:, dc, :],
                                              xn2[:, dc, 512 * hh:512 * (hh + 1)],
                                              start=(dc == 0), stop=(dc == NDC - 1)),
                                     inc=(tm if (dc == NDC - 1 and hh == 1) else None))
                        wsl_consumed[pa] = tm.n
                        p.w("a", tm, tm.n)
                        p.w("a", sk[pb], kst_stage_free[pb])
                        p.do("a", lambda e, ct=ct, pb=pb:
                             e.activation(out=kst[pb][:, :], in_=bank(2 * pb, 2), func=AF.Identity,
                                          bias=qkb_sb[:, ct:ct + 1]), inc=am)
                        qk_banks_free[pb] = am.n
                        p.w("a", am, am.n)
                        if ct >= 64:
                            h, tgt = ct - 64, vbufT
                        elif ct >= 32:
                            h, tgt = ct - 32, kbuf
                        else:
                            h, tgt = ct, qbuf
                        p.do("a", lambda e, h=h, t0=t0, pb=pb, tgt=tgt:
                             e.dma_start(out=tgt[h, :, t0:t0 + 1024], in_=kst[pb][:, :]), inc=sk[pb], k=16)
                        kst_stage_free[pb] = sk[pb].n
                    scoresT_free[0] = am.n
            qkv_stored = (sk[0].n, sk[1].n)
            p.barrier(bar)

        if "x1" in dump:
            p.do("s", lambda e: e.dma_start(out=dumps["d_x1"], in_=x1T), inc=dmp, k=16)
        if "kvq" in dump:
            p.do("s", lambda e: e.dma_start(out=dumps["d_k"], in_=kbuf), inc=dmp, k=16)
            p.do("s", lambda e: e.dma_start(out=dumps["d_v"], in_=vbufT), inc=dmp, k=16)
            p.do("s", lambda e: e.dma_start(out=dumps["d_q"], in_=qbuf), inc=dmp, k=16)
        if dump:
            p.w("s", dmp, dmp.n)
            p.barrier(bar)

        with ExitStack() as p34:
            ctx_all = p34.enter_context(nc.sbuf_tensor("ctx_all", [128, NH, TOWN], BF16))
            p3 = ExitStack()
            kst3_s = p3.enter_context(nc.sbuf_tensor("kst3", [128, TALL], BF16))
            kst3 = [kst3_s, kst3_s]
            vst3 = [p3.enter_context(nc.sbuf_tensor(f"vst3{i}", [128, TALL], BF16)) for i in range(2)]
            vtT = p3.enter_context(nc.sbuf_tensor("vtT", [128, TALL], BF16))
            qst3 = p3.enter_context(nc.sbuf_tensor("qst3", [128, TOWN], BF16))
            pt = p3.enter_context(nc.sbuf_tensor("pt", [128, 4, 1024], BF16))
            zacc = p3.enter_context(nc.sbuf_tensor("zacc", [128, 1024], F32))
            zb_sb = p3.enter_context(nc.sbuf_tensor("zb_sb", [128, 1024], F32))
            zrow = p3.enter_context(nc.sbuf_tensor("zrow", [1, 1024], F32))
            zrec = p3.enter_context(nc.sbuf_tensor("zrec", [1, 1024], F32))

            kv_consumed = [0, 0]
            k_consumed = [0]
            vt_evac = [0, 0]
            q_consumed = 0
            pt_slots = [(0, 0)] * 4
            st_free = [0, 0]
            ctx_free = 0
            z_banks_free = 0

            vt_done = [0]
            zb_evac = [0]
            p.w("s", sk[0], qkv_stored[0])
            p.w("s", sk[1], qkv_stored[1])
            for h in range(NH):
                pa = h % 2
                p.w("s", tm, k_consumed[0])
                p.do("s", lambda e, h=h, pa=pa: e.dma_start(out=kst3[pa][:, :], in_=kbuf[h, :, :]), inc=k3s, k=16)
                p.w("s", tm, kv_consumed[pa])
                p.w("s", am, vt_done[0])
                p.do("s", lambda e, h=h: e.dma_start(out=vtT[:, :], in_=vbufT[h, :, :]), inc=v3s, k=16)
                p.w("s", tm, q_consumed)
                p.do("s", lambda e, h=h: e.dma_start(out=qst3[:, :], in_=qbuf[h, :, :]), inc=q3s, k=16)
                p.w("t", k3s, k3s.n)
                p.w("t", v3s, v3s.n)
                p.w("t", q3s, q3s.n)
                p.w("t", vm, z_banks_free)
                p.w("t", am, zb_evac[0])
                for g8 in range(8):
                    zbk = 6 + (g8 % 2)
                    if g8 >= 2:
                        p.w("t", am, vt_evac[g8 % 2])
                    for ti in range(4):
                        kt = 4 * g8 + ti
                        p.do("t", lambda e, kt=kt, ti=ti, zbk=zbk:
                             e.matmul(bank(zbk)[:, 128 * ti:128 * (ti + 1)], vtT[:, 128 * kt:128 * (kt + 1)],
                                      id_bf[:, :], start=True, stop=True),
                             inc=(tm if ti == 3 else None))
                    p.w("a", tm, tm.n)
                    p.do("a", lambda e, g8=g8, pa=pa, zbk=zbk:
                         e.activation(out=vst3[pa][:, 512 * g8:512 * (g8 + 1)], in_=bank(zbk), func=AF.Identity),
                         inc=am)
                    vt_evac[g8 % 2] = am.n
                vt_done[0] = am.n
                for qc in range(2):
                    for kt in range(32):
                        stp = kt % 2
                        slot = kt % 4
                        p.w("t", am, st_free[stp])
                        for hh in range(2):
                            p.do("t", lambda e, kt=kt, qc=qc, hh=hh, stp=stp, pa=pa:
                                 e.matmul(bank(2 * stp + hh), kst3[pa][:, 128 * kt:128 * (kt + 1)],
                                          qst3[:, 1024 * qc + 512 * hh:1024 * qc + 512 * (hh + 1)],
                                          start=True, stop=True),
                                 inc=(tm if hh == 1 else None))
                        st_mm = tm.n
                        if qc == 1 and kt == 31:
                            q_consumed = tm.n
                            k_consumed[0] = tm.n
                        p.w("a", tm, st_mm)
                        p.w("a", vm, pt_slots[slot][0])
                        p.w("a", tm, pt_slots[slot][1])
                        p.do("a", lambda e, slot=slot, stp=stp:
                             e.activation(out=pt[:, slot, :], in_=bank(2 * stp, 2), func=AF.Exp,
                                          scale=ISQ, bias=cexp[:, :]), inc=am)
                        st_free[stp] = am.n
                        p.w("v", am, am.n)
                        if kt == 0:
                            p.w("v", vm, ctx_free)
                            p.do("v", lambda e, slot=slot: e.tensor_copy(out=zacc[:, :], in_=pt[:, slot, :]), inc=vm)
                        else:
                            p.w("v", vm, vm.n)
                            p.do("v", lambda e, slot=slot: e.tensor_tensor(out=zacc[:, :], in0=zacc[:, :],
                                                                           in1=pt[:, slot, :], op=ALU.add), inc=vm)
                        zadd = vm.n
                        p.w("t", am, am.n)
                        if kt == 0:
                            p.w("t", vm, ctx_free)
                            p.w("t", am, vt_done[0])
                        for hh in range(2):
                            p.do("t", lambda e, kt=kt, slot=slot, hh=hh, pa=pa:
                                 e.matmul(bank(4 + hh), vst3[pa][:, 128 * kt:128 * (kt + 1)],
                                          pt[:, slot, 512 * hh:512 * (hh + 1)],
                                          start=(kt == 0), stop=(kt == 31)),
                                 inc=(tm if hh == 1 else None))
                        pt_slots[slot] = (zadd, tm.n)
                        if qc == 1 and kt == 31:
                            kv_consumed[pa] = tm.n
                    pv_done = tm.n
                    p.w("t", vm, vm.n)
                    p.w("t", vm, z_banks_free)
                    p.do("t", lambda e: e.matmul(bank(6)[0:1, :], ones_f[:, :], zacc[:, 0:512], start=True, stop=True))
                    p.do("t", lambda e: e.matmul(bank(7)[0:1, :], ones_f[:, :], zacc[:, 512:1024], start=True, stop=True), inc=tm)
                    p.w("v", tm, tm.n)
                    p.do("v", lambda e: e.tensor_copy(out=zrow[0:1, 0:512], in_=bank(6)[0:1, :]))
                    p.do("v", lambda e: e.tensor_copy(out=zrow[0:1, 512:1024], in_=bank(7)[0:1, :]), inc=vm)
                    p.w("v", vm, vm.n)
                    p.do("v", lambda e: e.reciprocal(out=zrec[0:1, :], in_=zrow[0:1, :]), inc=vm)
                    p.w("t", vm, vm.n)
                    p.do("t", lambda e: e.matmul(bank(6), ones1[0:1, :], zrec[0:1, 0:512], start=True, stop=True))
                    p.do("t", lambda e: e.matmul(bank(7), ones1[0:1, :], zrec[0:1, 512:1024], start=True, stop=True), inc=tm)
                    p.w("a", tm, tm.n)
                    p.do("a", lambda e: e.activation(out=zb_sb[:, :], in_=bank(6, 2), func=AF.Identity), inc=am)
                    zb_evac[0] = am.n
                    p.w("v", am, am.n)
                    p.w("v", tm, pv_done)
                    p.do("v", lambda e, h=h, qc=qc: e.tensor_tensor(out=ctx_all[:, h, 1024 * qc:1024 * (qc + 1)],
                                                                    in0=bank(4, 2), in1=zb_sb[:, :], op=ALU.mult), inc=vm)
                    ctx_free = vm.n
                    z_banks_free = vm.n

            p.barrier(bar)
            p3.close()

            with ExitStack() as p4:
                owsl = [p4.enter_context(nc.sbuf_tensor(f"owsl{i}", [128, NDC, 128], BF16)) for i in range(2)]
                x1t = [p4.enter_context(nc.sbuf_tensor(f"x1t{i}", [128, TOWN], F32)) for i in range(2)]
                otmp = [p4.enter_context(nc.sbuf_tensor(f"otmp{i}", [128, TOWN], F32)) for i in range(2)]

                ow_consumed = [0, 0]
                o_banks_free = [0, 0]
                o_res = [0, 0]
                o_st = [0, 0]
                ctx_ready = vm.n
                p.w("s", sx[0], sx[0].n)
                p.w("s", sx[1], sx[1].n)
                for ct in range(32):
                    pa = ct % 2
                    p.w("s", tm, ow_consumed[pa])
                    p.do("s", lambda e, ct=ct, pa=pa:
                         e.dma_start(out=owsl[pa][:, :, :],
                                     in_=ow_bf.rearrange("(dc p) c -> p dc c", p=128)[:, :, 128 * ct:128 * (ct + 1)]),
                         inc=wo[pa], k=16)
                    p.w("s", vm, o_res[pa])
                    p.do("s", lambda e, ct=ct, pa=pa:
                         e.dma_start(out=x1t[pa][:, :], in_=x1T[128 * ct:128 * (ct + 1), 0:TOWN]), inc=xl[pa], k=16)
                    p.w("t", wo[pa], wo[pa].n)
                    p.w("t", vm, ctx_ready)
                    p.w("t", am, o_banks_free[pa])
                    for hh in range(NH):
                        for q4 in range(4):
                            p.do("t", lambda e, hh=hh, q4=q4, pa=pa:
                                 e.matmul(bank(4 * pa + q4), owsl[pa][:, hh, :], ctx_all[:, hh, 512 * q4:512 * (q4 + 1)],
                                          start=(hh == 0), stop=(hh == NH - 1)),
                                 inc=(tm if (hh == NH - 1 and q4 == 3) else None))
                    ow_consumed[pa] = tm.n
                    p.w("a", tm, tm.n)
                    p.w("a", vm, o_res[pa])
                    p.do("a", lambda e, ct=ct, pa=pa: e.activation(out=otmp[pa][:, :], in_=bank(4 * pa, 4),
                                                                   func=AF.Identity, bias=ob_sb[:, ct:ct + 1]), inc=am)
                    o_banks_free[pa] = am.n
                    p.w("v", am, am.n)
                    p.w("v", xl[pa], xl[pa].n)
                    p.w("v", s2[pa], o_st[pa])
                    p.do("v", lambda e, pa=pa: e.tensor_tensor(out=otmp[pa][:, :], in0=otmp[pa][:, :],
                                                               in1=x1t[pa][:, :], op=ALU.add), inc=vm)
                    o_res[pa] = vm.n
                    p.w("a", vm, vm.n)
                    p.do("a", lambda e, ct=ct, pa=pa:
                         e.dma_start(out=x2T[128 * ct:128 * (ct + 1), :], in_=otmp[pa][:, :]), inc=s2[pa], k=16)
                    o_st[pa] = s2[pa].n
                p.w("a", s2[0], s2[0].n)
                p.w("a", s2[1], s2[1].n)
                p.barrier(bar)
            x2_stored = (s2[0].n, s2[1].n)

        if "x2" in dump:
            p.do("s", lambda e: e.dma_start(out=dumps["d_x2"], in_=x2T), inc=dmp, k=16)
            p.w("s", dmp, dmp.n)
            p.barrier(bar)

        with ExitStack() as p5:
            buf["xf"] = [p5.enter_context(nc.sbuf_tensor(f"xf5_{i}", [128, NDC, 128], F32)) for i in range(2)]
            buf["xb2"] = [p5.enter_context(nc.sbuf_tensor(f"xb5_{i}", [128, NDC, 128], BF16)) for i in range(2)]
            buf["xq2"] = p5.enter_context(nc.sbuf_tensor("xq25", [128, NDC, 128], BF16))
            buf["rows"] = p5.enter_context(nc.sbuf_tensor("rows5", [1, 6, 128], F32))
            buf["hg"] = p5.enter_context(nc.sbuf_tensor("hg5", [128, 8, 512], BF16))
            buf["fslab"] = [p5.enter_context(nc.sbuf_tensor(f"fslab5{i}", [128, 8192], BF16)) for i in range(2)]
            buf["xres"] = [p5.enter_context(nc.sbuf_tensor(f"xres5{i}", [128, 512], F32)) for i in range(2)]
            buf["xout"] = [p5.enter_context(nc.sbuf_tensor(f"xout5{i}", [128, 512], F32)) for i in range(2)]
            buf["snat"] = p5.enter_context(nc.sbuf_tensor("snat5", [128, 4, 64], F32))
            buf["sT_sb"] = p5.enter_context(nc.sbuf_tensor("sT_sb5", [64, 512], F32))
            buf["GT_sb"] = p5.enter_context(nc.sbuf_tensor("GT_sb5", [64, 512], F32))
            buf["gx_sb"] = p5.enter_context(nc.sbuf_tensor("gx_sb5", [128, 512], BF16))
            xn5 = p5.enter_context(nc.sbuf_tensor("xn5", [128, NDC, 512], BF16))

            src_ready[0] = [(s2[0], x2_stored[0]), (s2[1], x2_stored[1])]
            p.w("s", s2[0], x2_stored[0])
            p.w("s", s2[1], x2_stored[1])
            for w5 in range(4):
                t0 = 512 * w5
                emit_ln(x2T, t0, xn5, 0)
                emit_ffn(xn5, t0, x2T, outT, t0)

        p.w("s", sx[0], sx[0].n)
        p.w("s", sx[1], sx[1].n)
        p.barrier(bar)
        p.flush()
    return nc, dumps


_NC_CACHE = {}


def _get_nc(dump=()):
    key = tuple(sorted(dump))
    if key not in _NC_CACHE:
        _NC_CACHE[key] = build(dump)
    return _NC_CACHE[key]


def _host_inputs(L0_attn_x, L0_attn_attn_qkvw, L0_attn_attn_qkvb, L0_attn_attn_ow,
                 L0_attn_attn_ob, L0_ffn_A, L0_ffn_B, L0_ffn_Wr):
    x = np.asarray(L0_attn_x, np.float32)
    wq = np.ascontiguousarray(np.asarray(L0_attn_attn_qkvw, np.float32))
    qkvb = np.asarray(L0_attn_attn_qkvb, np.float32)
    owh = np.ascontiguousarray(np.asarray(L0_attn_attn_ow, np.float32))
    obh = np.asarray(L0_attn_attn_ob, np.float32)
    A = np.asarray(L0_ffn_A, np.float32)
    B = np.asarray(L0_ffn_B, np.float32)
    Wr = np.asarray(L0_ffn_Wr, np.float32)

    wrT = np.ascontiguousarray(Wr.T)
    aTh = np.ascontiguousarray(np.transpose(A, (2, 0, 1)).reshape(D, 1024))
    bfh = np.ascontiguousarray(np.transpose(B, (0, 2, 1)).reshape(1024, D))
    qkbh = np.ascontiguousarray(qkvb.reshape(96, 128).T)
    obrh = np.ascontiguousarray(obh.reshape(32, 128).T)
    em = np.zeros((64, 1024), np.float32)
    em[np.arange(1024) >> 4, np.arange(1024)] = 1.0
    idh = np.eye(128, dtype=np.float32)

    shared = dict(wqkv=wq, ow=owh, wrT=wrT, aT=aTh, bfm=bfh, qkb=qkbh,
                  obr=obrh, emat=em, idf=idh)
    in_maps = []
    for c in range(8):
        b, hf = c // 2, c % 2
        own = x[b, 2048 * hf:2048 * (hf + 1), :]
        sib = x[b, 2048 * (1 - hf):2048 * (2 - hf), :]
        xTh = np.ascontiguousarray(np.concatenate([own, sib], 0).T)
        in_maps.append(dict(shared, xT=xTh))
    return in_maps


def kernel(L0_attn_x, L0_attn_attn_qkvw, L0_attn_attn_qkvb, L0_attn_attn_ow,
           L0_attn_attn_ob, L0_ffn_A, L0_ffn_B, L0_ffn_Wr):
    in_maps = _host_inputs(L0_attn_x, L0_attn_attn_qkvw, L0_attn_attn_qkvb,
                           L0_attn_attn_ow, L0_attn_attn_ob, L0_ffn_A, L0_ffn_B, L0_ffn_Wr)
    nc, _ = _get_nc()
    res = run_bass_kernel_spmd(nc, in_maps, core_ids=list(range(8)))
    out = np.empty((4, 4096, 4096), np.float32)
    for c in range(8):
        b, hf = c // 2, c % 2
        out[b, 2048 * hf:2048 * (hf + 1), :] = res.results[c]["outT"].T
    return out


# version 11 (final)
# speedup vs baseline: 1.0333x; 1.0158x over previous
import numpy as np
import ml_dtypes
from contextlib import ExitStack
import concourse.bass as bass
import concourse.mybir as mybir
from concourse.bass_utils import run_bass_kernel_spmd

F32 = mybir.dt.float32
BF16 = mybir.dt.bfloat16
AF = mybir.ActivationFunctionType
ALU = mybir.AluOpType
AX = mybir.AxisListType
bfdt = ml_dtypes.bfloat16

D = 4096
NDC = 32
TALL = 4096
TOWN = 2048
NH = 32
HD = 128
EPS = 1e-5
ISQ = float(1.0 / np.sqrt(128.0))
EXPC = 4.0


class S:
    def __init__(self, h):
        self.h = h
        self.n = 0


class P:
    def __init__(self, nc):
        self.nc = nc
        self.ops = {k: [] for k in "tvags"}
        self.nbar = 0

    def w(self, eng, sem, val):
        if val > 0:
            self.ops[eng].append(lambda e, h=sem.h, v=val: e.wait_ge(h, v))

    def do(self, eng, fn, inc=None, k=1):
        if inc is None:
            self.ops[eng].append(fn)
            return None
        inc.n += k
        self.ops[eng].append(lambda e, fn=fn, h=inc.h, k=k: fn(e).then_inc(h, k))
        return inc.n

    def barrier(self, bar):
        self.nbar += 1
        tgt = 5 * self.nbar
        for eng in "tvags":
            bar.n += 1
            if eng == "s":
                self.ops[eng].append(lambda e, h=bar.h: e.sem_inc(h, 1))
            else:
                self.ops[eng].append(lambda e, h=bar.h: e.drain().then_inc(h, 1))
            self.ops[eng].append(lambda e, h=bar.h, t=tgt: e.wait_ge(h, t))

    def flush(self):
        with self.nc.Block() as blk:
            blk.tensor(lambda e: [f(e) for f in self.ops["t"]])
            blk.vector(lambda e: [f(e) for f in self.ops["v"]])
            blk.scalar(lambda e: [f(e) for f in self.ops["a"]])
            blk.gpsimd(lambda e: [f(e) for f in self.ops["g"]])
            blk.sync(lambda e: [f(e) for f in self.ops["s"]])


def rearr_dT(ap, t0, tw):
    return ap.rearrange("(dc p) t -> p dc t", p=128)[:, :, t0:t0 + tw]


def bcast_dc(ap2d, ndc=NDC):
    a = ap2d
    return bass.AP(a.tensor, a.offset, [list(a.ap[0]), [0, ndc], list(a.ap[1])])


def build(dump=(), paranoid=False, skip_attn=False):
    nc = bass.Bass("TRN2", target_bir_lowering=False)

    xT = nc.dram_tensor("xT", [D, TALL], F32, kind="ExternalInput").ap()
    wqkv = nc.dram_tensor("wqkv", [D, 3 * D], F32, kind="ExternalInput").ap()
    ow = nc.dram_tensor("ow", [D, D], F32, kind="ExternalInput").ap()
    wrT = nc.dram_tensor("wrT", [D, 64], F32, kind="ExternalInput").ap()
    aT = nc.dram_tensor("aT", [D, 1024], F32, kind="ExternalInput").ap()
    bfm = nc.dram_tensor("bfm", [1024, D], F32, kind="ExternalInput").ap()
    qkb = nc.dram_tensor("qkb", [128, 96], F32, kind="ExternalInput").ap()
    obr = nc.dram_tensor("obr", [128, 32], F32, kind="ExternalInput").ap()
    emat = nc.dram_tensor("emat", [64, 1024], F32, kind="ExternalInput").ap()
    idf = nc.dram_tensor("idf", [128, 128], F32, kind="ExternalInput").ap()

    outT = nc.dram_tensor("outT", [D, TOWN], F32, kind="ExternalOutput").ap()

    wqkv_bf = nc.dram_tensor("wqkv_bf", [D, 3 * D], BF16, kind="Internal").ap()
    ow_bf = nc.dram_tensor("ow_bf", [D, D], BF16, kind="Internal").ap()
    aT_bf = nc.dram_tensor("aT_bf", [D, 1024], BF16, kind="Internal").ap()
    bf_bf = nc.dram_tensor("bf_bf", [1024, D], BF16, kind="Internal").ap()
    x1T = nc.dram_tensor("x1T", [D, TALL], F32, kind="Internal").ap()
    x2T = nc.dram_tensor("x2T", [D, TOWN], F32, kind="Internal").ap()
    kbuf = nc.dram_tensor("kbuf", [NH, HD, TALL], BF16, kind="Internal").ap()
    vbufT = nc.dram_tensor("vbufT", [NH, HD, TALL], BF16, kind="Internal").ap()
    qbuf = nc.dram_tensor("qbuf", [NH, HD, TOWN], BF16, kind="Internal").ap()

    dumps = {}
    if "x1" in dump:
        dumps["d_x1"] = nc.dram_tensor("d_x1", [D, TALL], F32, kind="ExternalOutput").ap()
    if "kvq" in dump:
        dumps["d_k"] = nc.dram_tensor("d_k", [NH, HD, TALL], BF16, kind="ExternalOutput").ap()
        dumps["d_v"] = nc.dram_tensor("d_v", [NH, HD, TALL], BF16, kind="ExternalOutput").ap()
        dumps["d_q"] = nc.dram_tensor("d_q", [NH, HD, TOWN], BF16, kind="ExternalOutput").ap()
    if "x2" in dump:
        dumps["d_x2"] = nc.dram_tensor("d_x2", [D, TOWN], F32, kind="ExternalOutput").ap()

    es = ExitStack()
    with es:
        ones_bf = es.enter_context(nc.sbuf_tensor("ones_bf", [128, 1], BF16))
        ones_f = es.enter_context(nc.sbuf_tensor("ones_f", [128, 1], F32))
        ones1 = es.enter_context(nc.sbuf_tensor("ones1", [1, 128], F32))
        wr_sb = es.enter_context(nc.sbuf_tensor("wr_sb", [128, NDC, 64], BF16))
        e_sb = es.enter_context(nc.sbuf_tensor("e_sb", [64, 1024], F32))
        id_sb = es.enter_context(nc.sbuf_tensor("id_sb", [128, 128], F32))
        qkb_sb = es.enter_context(nc.sbuf_tensor("qkb_sb", [128, 96], F32))
        id_bf = es.enter_context(nc.sbuf_tensor("id_bf", [128, 128], BF16))
        ob_sb = es.enter_context(nc.sbuf_tensor("ob_sb", [128, 32], F32))
        rows_holder = {}
        epsb = es.enter_context(nc.sbuf_tensor("epsb", [1, 1], F32))
        cexp = es.enter_context(nc.sbuf_tensor("cexp", [128, 1], F32))
        m8a = es.enter_context(nc.sbuf_tensor("m8a", [128, 8], F32))
        m8b = es.enter_context(nc.sbuf_tensor("m8b", [128, 8], F32))
        ntau = es.enter_context(nc.sbuf_tensor("ntau", [128, 1], F32))
        mrs = es.enter_context(nc.sbuf_tensor("mrs", [128, 64], F32))
        etile = es.enter_context(nc.sbuf_tensor("etile", [128, 64], F32))
        gsum = es.enter_context(nc.sbuf_tensor("gsum", [128, 1], F32))
        grec = es.enter_context(nc.sbuf_tensor("grec", [128, 1], F32))
        gtile = es.enter_context(nc.sbuf_tensor("gtile", [128, 64], F32))

        PS = es.enter_context(nc.psum_tensor("PS", [128, 4096], F32))

        def bank(i, n=1):
            return PS[:, 512 * i:512 * (i + n)]

        tm = S(es.enter_context(nc.semaphore("tm")))
        vm = S(es.enter_context(nc.semaphore("vm")))
        am = S(es.enter_context(nc.semaphore("am")))
        bar = S(es.enter_context(nc.semaphore("bar")))
        cst = S(es.enter_context(nc.semaphore("cst")))
        lnx = [S(es.enter_context(nc.semaphore(f"lnx{i}"))) for i in range(2)]
        dmp = S(es.enter_context(nc.semaphore("dmp")))
        c_ = [S(es.enter_context(nc.semaphore(f"c{i}"))) for i in range(2)]
        sc = [S(es.enter_context(nc.semaphore(f"sc{i}"))) for i in range(2)]
        fs = [S(es.enter_context(nc.semaphore(f"fs{i}"))) for i in range(2)]
        xr = [S(es.enter_context(nc.semaphore(f"xr{i}"))) for i in range(2)]
        sx = [S(es.enter_context(nc.semaphore(f"sx{i}"))) for i in range(2)]
        ws = [S(es.enter_context(nc.semaphore(f"ws{i}"))) for i in range(2)]
        sk = [S(es.enter_context(nc.semaphore(f"sk{i}"))) for i in range(2)]
        k3s = S(es.enter_context(nc.semaphore("k3s")))
        v3s = S(es.enter_context(nc.semaphore("v3s")))
        q3s = S(es.enter_context(nc.semaphore("q3s")))
        wo = [S(es.enter_context(nc.semaphore(f"wo{i}"))) for i in range(2)]
        xl = [S(es.enter_context(nc.semaphore(f"xl{i}"))) for i in range(2)]
        s2 = [S(es.enter_context(nc.semaphore(f"s2{i}"))) for i in range(2)]

        p = P(nc)

        ln_free = [0, 0]
        ln_q_free = [0]
        ln_banks_free = [0]
        mq_evac = [0]
        norm_read = [0, 0]
        b45_free = [0]
        b23_free = [0]
        src_ready = [[]]
        scoresT_free = [0]
        slab_consumed = [0, 0]
        gexp_evac = [0]
        gx_read = [0]
        gT_read = [0]
        hg_free = [0, 0]
        res_done = [0, 0]
        out_stored = [0, 0]
        buf = {}

        def emit_ln(src, t0, xnbf, off):
            for (s_, v_) in src_ready[0]:
                p.w("s", s_, v_)
            r_done = [0, 0, 0, 0]
            def _ln_a(sci):
                lp = sci % 2
                xf, xb2, xq2 = buf["xf"][lp], buf["xb2"][lp], buf["xq2"]
                mrow, rrow = buf["mrow"], buf["rrow"]
                qrow, trow = buf["qrow"], buf["trow"]
                tt0 = t0 + 128 * sci
                p.w("s", vm, ln_free[lp])
                p.do("s", lambda e, src=src, tt0=tt0, xf=xf:
                     e.dma_start(out=xf[:, :, :], in_=rearr_dT(src, tt0, 128)), inc=lnx[lp], k=16)
                p.w("v", lnx[lp], lnx[lp].n)
                p.do("v", lambda e, xf=xf, xb2=xb2: e.tensor_copy(out=xb2[:, :, :], in_=xf[:, :, :]), inc=vm)
                p.w("v", vm, vm.n)
                p.w("v", tm, ln_q_free[0])
                p.do("v", lambda e, xb2=xb2, xq2=xq2:
                     e.tensor_mul(out=xq2[:, :, :], in0=xb2[:, :, :], in1=xb2[:, :, :]), inc=vm)
                p.w("t", vm, vm.n)
                p.w("t", vm, mq_evac[0])
                if sci == 0:
                    p.w("t", vm, ln_banks_free[0])
                for dc in range(NDC):
                    p.do("t", lambda e, dc=dc, xb2=xb2, sci=sci:
                         e.matmul(bank(6)[0:1, 128 * sci:128 * (sci + 1)], ones_bf[:, :], xb2[:, dc, :],
                                  start=(dc == 0), stop=(dc == NDC - 1)))
                for dc in range(NDC):
                    p.do("t", lambda e, dc=dc, xq2=xq2, sci=sci:
                         e.matmul(bank(7)[0:1, 128 * sci:128 * (sci + 1)], ones_bf[:, :], xq2[:, dc, :],
                                  start=(dc == 0), stop=(dc == NDC - 1)),
                         inc=(tm if dc == NDC - 1 else None))
                ln_q_free[0] = tm.n
                p.w("v", tm, tm.n)
                p.do("v", lambda e, sci=sci, mrow=mrow: e.tensor_scalar_mul(
                    out=mrow[0:1, sci, :], in0=bank(6)[0:1, 128 * sci:128 * (sci + 1)], scalar1=1.0 / D))
                p.do("v", lambda e, sci=sci, lp=lp, qrow=qrow: e.tensor_scalar_mul(
                    out=qrow[0:1, lp, :], in0=bank(7)[0:1, 128 * sci:128 * (sci + 1)], scalar1=1.0 / D), inc=vm)
                mq_evac[0] = vm.n
                p.w("v", vm, vm.n)
                p.do("v", lambda e, sci=sci, lp=lp, mrow=mrow, trow=trow: e.tensor_mul(
                    out=trow[0:1, lp, :], in0=mrow[0:1, sci, :], in1=mrow[0:1, sci, :]), inc=vm)
                p.w("v", vm, vm.n)
                p.do("v", lambda e, lp=lp, qrow=qrow, trow=trow: e.tensor_sub(
                    out=qrow[0:1, lp, :], in0=qrow[0:1, lp, :], in1=trow[0:1, lp, :]), inc=vm)
                p.w("a", vm, vm.n)
                p.do("a", lambda e, lp=lp, qrow=qrow, trow=trow: e.activation(
                    out=trow[0:1, lp, :], in_=qrow[0:1, lp, :], func=AF.Ln, bias=epsb[0:1, :]), inc=am)
                p.w("a", am, am.n)
                p.do("a", lambda e, sci=sci, lp=lp, trow=trow, rrow=rrow: e.activation(
                    out=rrow[0:1, sci, :], in_=trow[0:1, lp, :], func=AF.Exp, scale=-0.5), inc=am)
                r_done[sci] = am.n
            def _ln_b(sci):
                lp = sci % 2
                xf = buf["xf"][lp]
                mrow, rrow = buf["mrow"], buf["rrow"]
                bm = 4 if sci % 2 == 0 else 2
                sl = 128 * (sci // 2)
                p.w("t", am, r_done[sci])
                p.w("t", vm, norm_read[sci % 2])
                p.w("t", am, b45_free[0] if sci % 2 == 0 else b23_free[0])
                p.do("t", lambda e, sci=sci, mrow=mrow, bm=bm, sl=sl: e.matmul(
                    bank(bm)[:, sl:sl + 128], ones1[0:1, :], mrow[0:1, sci, :], start=True, stop=True))
                p.do("t", lambda e, sci=sci, rrow=rrow, bm=bm, sl=sl: e.matmul(
                    bank(bm + 1)[:, sl:sl + 128], ones1[0:1, :], rrow[0:1, sci, :], start=True, stop=True), inc=tm)
                p.w("v", tm, tm.n)
                p.do("v", lambda e, xf=xf, bm=bm, sl=sl:
                     e.tensor_tensor(out=xf[:, :, :], in0=xf[:, :, :],
                                     in1=bcast_dc(bank(bm)[:, sl:sl + 128]), op=ALU.subtract), inc=vm)
                p.w("v", vm, vm.n)
                p.do("v", lambda e, xnbf=xnbf, off=off, sc=sci, xf=xf, bm=bm, sl=sl:
                     e.tensor_tensor(out=xnbf[:, :, off + 128 * sc:off + 128 * (sc + 1)],
                                     in0=xf[:, :, :], in1=bcast_dc(bank(bm + 1)[:, sl:sl + 128]),
                                     op=ALU.mult), inc=vm)
                ln_free[lp] = vm.n
                norm_read[sci % 2] = vm.n
            for step in range(6):
                if step >= 2:
                    _ln_b(step - 2)
                if step < 4:
                    _ln_a(step)
            ln_banks_free[0] = vm.n

        def emit_ffn(xnbf, t0, resid_src, dst, dst_t0):
            hg, fslab, xres, xout = buf["hg"], buf["fslab"], buf["xres"], buf["xout"]
            p.w("t", vm, vm.n)
            p.w("t", am, scoresT_free[0])
            for dc in range(NDC):
                p.do("t", lambda e, dc=dc, xnbf=xnbf:
                     e.matmul(bank(3)[0:64, :], wr_sb[:, dc, :], xnbf[:, dc, :],
                              start=(dc == 0), stop=(dc == NDC - 1)),
                     inc=(tm if dc == NDC - 1 else None))
            p.w("a", tm, tm.n)
            p.do("a", lambda e: e.activation(out=buf['sT_sb'][:, :], in_=bank(3)[0:64, :], func=AF.Identity), inc=am)
            scoresT_free[0] = am.n
            p.w("t", am, am.n)
            for tt in range(4):
                p.do("t", lambda e, tt=tt: e.transpose(bank(4)[:, 64 * tt:64 * (tt + 1)],
                                                       buf['sT_sb'][:, 128 * tt:128 * (tt + 1)], id_sb[0:64, 0:64]),
                     inc=(tm if tt == 3 else None))
            p.w("a", tm, tm.n)
            p.do("a", lambda e: e.activation(out=buf['snat'][:, :, :], in_=bank(4)[:, 0:256], func=AF.Identity), inc=am)
            for tt in range(4):
                p.w("v", am, am.n)
                p.do("v", lambda e, tt=tt: e.max(out=m8a[:, :], in_=buf['snat'][:, tt, :]), inc=vm)
                p.w("v", vm, vm.n)
                p.do("v", lambda e, tt=tt: e.match_replace(out=mrs[:, :], in_to_replace=m8a[:, :],
                                                           in_values=buf['snat'][:, tt, :], imm_value=-1e30), inc=vm)
                p.w("v", vm, vm.n)
                p.do("v", lambda e: e.max(out=m8b[:, :], in_=mrs[:, :]), inc=vm)
                p.w("v", vm, vm.n)
                p.do("v", lambda e: e.tensor_scalar_mul(out=ntau[:, :], in0=m8b[:, 7:8], scalar1=-1.0), inc=vm)
                p.w("a", vm, vm.n)
                p.do("a", lambda e, tt=tt: e.activation(out=etile[:, :], in_=buf['snat'][:, tt, :], func=AF.Exp,
                                                        bias=ntau[:, :]), inc=am)
                p.w("v", am, am.n)
                p.do("v", lambda e, tt=tt: e.tensor_scalar(out=mrs[:, :], in0=buf['snat'][:, tt, :],
                                                           scalar1=m8b[:, 7:8], scalar2=None, op0=ALU.is_ge), inc=vm)
                p.w("v", vm, vm.n)
                p.do("v", lambda e: e.tensor_mul(out=etile[:, :], in0=etile[:, :], in1=mrs[:, :]), inc=vm)
                p.w("v", vm, vm.n)
                p.do("v", lambda e: e.reduce_sum(out=gsum[:, :], in_=etile[:, :], axis=AX.X), inc=vm)
                p.w("v", vm, vm.n)
                p.do("v", lambda e: e.reciprocal(out=grec[:, :], in_=gsum[:, :]), inc=vm)
                p.w("v", vm, vm.n)
                p.w("v", tm, gT_read[0])
                p.do("v", lambda e: e.tensor_scalar(out=gtile[:, :], in0=etile[:, :], scalar1=grec[:, :],
                                                    scalar2=None, op0=ALU.mult), inc=vm)
                p.w("t", vm, vm.n)
                p.do("t", lambda e, tt=tt: e.transpose(bank(5)[0:64, 128 * tt:128 * (tt + 1)],
                                                       gtile[:, :], id_sb[:, :]), inc=tm)
                gT_read[0] = tm.n
            p.w("a", tm, tm.n)
            p.do("a", lambda e: e.activation(out=buf['GT_sb'][:, :], in_=bank(5)[0:64, :], func=AF.Identity), inc=am)
            gt_ready = am.n
            b45_free[0] = am.n

            for j in range(8):
                pa = j % 2
                j01 = j % 2
                p.w("s", tm, slab_consumed[pa])
                p.do("s", lambda e, j=j, pa=pa, fslab=fslab:
                     e.dma_start(out=fslab[pa][:, :].rearrange("p (dc c) -> p dc c", c=128),
                                 in_=aT_bf.rearrange("(dc p) j -> p dc j", p=128)[:, :, 128 * j:128 * (j + 1)]),
                     inc=fs[pa], k=16)
                p.w("t", fs[pa], fs[pa].n)
                if j == 0:
                    p.w("t", am, gt_ready)
                p.w("t", vm, hg_free[j01])
                for dc in range(NDC):
                    p.do("t", lambda e, dc=dc, pa=pa, fslab=fslab, xnbf=xnbf:
                         e.matmul(bank(j01 if False else (pa % 2)),
                                  fslab[pa][:, :].rearrange("p (dc c) -> p dc c", c=128)[:, dc, :],
                                  xnbf[:, dc, :], start=(dc == 0), stop=(dc == NDC - 1)))
                p.w("t", am, gexp_evac[0])
                p.do("t", lambda e, j=j: e.matmul(bank(2), e_sb[:, 128 * j:128 * (j + 1)], buf['GT_sb'][:, :],
                                                  start=True, stop=True), inc=tm)
                slab_consumed[pa] = tm.n
                p.w("a", tm, tm.n)
                p.w("a", vm, gx_read[0])
                p.do("a", lambda e: e.activation(out=buf['gx_sb'][:, :], in_=bank(2), func=AF.Identity), inc=am)
                gexp_evac[0] = am.n
                p.w("v", am, am.n)
                p.do("v", lambda e, j=j, j01=j01, hg=hg:
                     e.tensor_tensor(out=hg[:, j, :], in0=bank(j01), in1=buf['gx_sb'][:, :], op=ALU.mult), inc=vm)
                hg_free[j01] = vm.n
                gx_read[0] = vm.n
            b23_free[0] = am.n
            hg_all = vm.n

            for ctg in range(8):
                pa = ctg % 2
                p.w("s", tm, slab_consumed[pa])
                p.do("s", lambda e, ctg=ctg, pa=pa, fslab=fslab:
                     e.dma_start(out=fslab[pa][:, :].rearrange("p (j c) -> p j c", c=512),
                                 in_=bf_bf.rearrange("(j p) c -> p j c", p=128)[:, :, 512 * ctg:512 * (ctg + 1)]),
                     inc=fs[pa], k=16)
                p.w("t", fs[pa], fs[pa].n)
                p.w("t", vm, hg_all)
                for ct8 in range(4):
                    ct = 4 * ctg + ct8
                    pb = ct8 % 2
                    p.w("t", vm, res_done[pb])
                    for j in range(8):
                        p.do("t", lambda e, j=j, ct8=ct8, pa=pa, pb=pb, fslab=fslab, hg=hg:
                             e.matmul(bank(pb),
                                      fslab[pa][:, :].rearrange("p (j c) -> p j c", c=512)[:, j, 128 * ct8:128 * (ct8 + 1)],
                                      hg[:, j, :], start=(j == 0), stop=(j == 7)),
                             inc=(tm if j == 7 else None))
                    slab_consumed[pa] = tm.n
                    p.w("s", vm, res_done[pb])
                    p.do("s", lambda e, ct=ct, t0=t0, pb=pb, resid_src=resid_src, xres=xres:
                         e.dma_start(out=xres[pb][:, :], in_=resid_src[128 * ct:128 * (ct + 1), t0:t0 + 512]),
                         inc=xr[pb], k=16)
                    p.w("v", tm, tm.n)
                    p.w("v", xr[pb], xr[pb].n)
                    p.w("v", sx[pb], out_stored[pb])
                    p.do("v", lambda e, pb=pb, xres=xres, xout=xout:
                         e.tensor_tensor(out=xout[pb][:, :], in0=bank(pb), in1=xres[pb][:, :], op=ALU.add), inc=vm)
                    res_done[pb] = vm.n
                    p.w("a", vm, vm.n)
                    p.do("a", lambda e, ct=ct, dst=dst, dst_t0=dst_t0, pb=pb, xout=xout:
                         e.dma_start(out=dst[128 * ct:128 * (ct + 1), dst_t0:dst_t0 + 512], in_=xout[pb][:, :]),
                         inc=sx[pb], k=16)
                    out_stored[pb] = sx[pb].n
            p.w("a", sx[0], sx[0].n)
            p.w("a", sx[1], sx[1].n)
            if paranoid:
                p.barrier(bar)

        p.do("v", lambda e: e.memset(ones_bf[:, :], 1.0))
        p.do("v", lambda e: e.memset(ones_f[:, :], 1.0))
        p.do("v", lambda e: e.memset(ones1[:, :], 1.0))
        p.do("v", lambda e: e.memset(epsb[:, :], EPS))
        p.do("v", lambda e: e.memset(cexp[:, :], -EXPC), inc=vm)
        p.do("s", lambda e: e.dma_start(out=e_sb[:, :], in_=emat), inc=cst, k=16)
        p.do("s", lambda e: e.dma_start(out=id_sb[:, :], in_=idf), inc=cst, k=16)
        p.do("s", lambda e: e.dma_start(out=qkb_sb[:, :], in_=qkb), inc=cst, k=16)
        p.do("s", lambda e: e.dma_start(out=ob_sb[:, :], in_=obr), inc=cst, k=16)
        p.w("v", cst, cst.n)
        p.do("v", lambda e: e.tensor_copy(out=id_bf[:, :], in_=id_sb[:, :]), inc=vm)

        if True:
            with ExitStack() as p0:
                cf = [p0.enter_context(nc.sbuf_tensor(f"cf{i}", [128, 3 * D], F32)) for i in range(2)]
                cb = [p0.enter_context(nc.sbuf_tensor(f"cb{i}", [128, 3 * D], BF16)) for i in range(2)]
                conv = [(wqkv, wqkv_bf, 32, 3 * D), (ow, ow_bf, 32, D),
                        (aT, aT_bf, 32, 1024), (bfm, bf_bf, 8, D), (wrT, None, 32, 64)]
                last_cvt = [0, 0]
                last_st = [0, 0]
                for (src, dst, nsl, cols) in conv:
                    for i in range(nsl):
                        pa = i % 2
                        p.w("s", vm, last_cvt[pa])
                        p.do("s", lambda e, src=src, i=i, cols=cols, pa=pa:
                             e.dma_start(out=cf[pa][:, 0:cols], in_=src[128 * i:128 * (i + 1), :]), inc=c_[pa], k=16)
                        p.w("v", c_[pa], c_[pa].n)
                        p.w("v", sc[pa], last_st[pa])
                        if dst is None:
                            p.do("v", lambda e, i=i, pa=pa:
                                 e.tensor_copy(out=wr_sb[:, i, :], in_=cf[pa][:, 0:64]), inc=vm)
                            last_cvt[pa] = vm.n
                        else:
                            p.do("v", lambda e, pa=pa, cols=cols:
                                 e.tensor_copy(out=cb[pa][:, 0:cols], in_=cf[pa][:, 0:cols]), inc=vm)
                            last_cvt[pa] = vm.n
                            p.w("a", vm, vm.n)
                            p.do("a", lambda e, dst=dst, i=i, cols=cols, pa=pa:
                                 e.dma_start(out=dst[128 * i:128 * (i + 1), :], in_=cb[pa][:, 0:cols]),
                                 inc=sc[pa], k=16)
                            last_st[pa] = sc[pa].n
                p.w("a", sc[0], sc[0].n)
                p.w("a", sc[1], sc[1].n)
                conv_done = (sc[0].n, sc[1].n)
                p.barrier(bar)

            with ExitStack() as p1:
                buf["xf"] = [p1.enter_context(nc.sbuf_tensor(f"xf_{i}", [128, NDC, 128], F32)) for i in range(2)]
                buf["xb2"] = [p1.enter_context(nc.sbuf_tensor(f"xb2_{i}", [128, NDC, 128], BF16)) for i in range(2)]
                buf["xq2"] = p1.enter_context(nc.sbuf_tensor("xq2", [128, NDC, 128], BF16))
                buf["mrow"] = p1.enter_context(nc.sbuf_tensor("mrow", [1, 4, 128], F32))
                buf["rrow"] = p1.enter_context(nc.sbuf_tensor("rrow", [1, 4, 128], F32))
                buf["qrow"] = p1.enter_context(nc.sbuf_tensor("qrow", [1, 2, 128], F32))
                buf["trow"] = p1.enter_context(nc.sbuf_tensor("trow", [1, 2, 128], F32))
                buf["hg"] = p1.enter_context(nc.sbuf_tensor("hg", [128, 8, 512], BF16))
                buf["fslab"] = [p1.enter_context(nc.sbuf_tensor(f"fslab{i}", [128, 4096], BF16)) for i in range(2)]
                buf["xres"] = [p1.enter_context(nc.sbuf_tensor(f"xres{i}", [128, 512], F32)) for i in range(2)]
                buf["xout"] = [p1.enter_context(nc.sbuf_tensor(f"xout{i}", [128, 512], F32)) for i in range(2)]
                buf["snat"] = p1.enter_context(nc.sbuf_tensor("snat", [128, 4, 64], F32))
                buf["sT_sb"] = p1.enter_context(nc.sbuf_tensor("sT_sb", [64, 512], F32))
                buf["GT_sb"] = p1.enter_context(nc.sbuf_tensor("GT_sb", [64, 512], F32))
                buf["gx_sb"] = p1.enter_context(nc.sbuf_tensor("gx_sb", [128, 512], BF16))
                xn2 = p1.enter_context(nc.sbuf_tensor("xn2", [128, NDC, 1024], BF16))
                xn1 = xn2[:, :, 512:1024]
                wsl = [p1.enter_context(nc.sbuf_tensor(f"wsl{i}", [128, NDC, 128], BF16)) for i in range(2)]
                kst = [p1.enter_context(nc.sbuf_tensor(f"kst{i}", [128, 1024], BF16)) for i in range(2)]

                wsl_consumed = [0, 0]
                qk_banks_free = [0, 0]
                kst_stage_free = [0, 0]

                p.w("s", sc[0], conv_done[0])
                p.w("s", sc[1], conv_done[1])
                p.w("s", cst, cst.n)
                for W in range(4):
                    for half in range(2):
                        t0 = 1024 * W + 512 * half
                        src_ready[0] = []
                        emit_ln(xT, t0, xn1, 0)
                        emit_ffn(xn1, t0, xT, x1T, t0)
                        src_ready[0] = [(sx[0], sx[0].n), (sx[1], sx[1].n)]
                        emit_ln(x1T, t0, xn2, 512 * half)
                    t0 = 1024 * W
                    own = t0 < TOWN
                    xn2_ready = vm.n
                    p.w("t", vm, xn2_ready)
                    cts = list(range(32, 96)) + (list(range(0, 32)) if own else [])
                    for ci, ct in enumerate(cts):
                        pa = ci % 2
                        pb = ci % 2
                        p.w("s", tm, wsl_consumed[pa])
                        p.do("s", lambda e, ct=ct, pa=pa:
                             e.dma_start(out=wsl[pa][:, :, :],
                                         in_=wqkv_bf.rearrange("(dc p) c -> p dc c", p=128)[:, :, 128 * ct:128 * (ct + 1)]),
                             inc=ws[pa], k=16)
                        p.w("t", ws[pa], ws[pa].n)
                        p.w("t", am, qk_banks_free[pb])
                        for dc in range(NDC):
                            for hh in range(2):
                                p.do("t", lambda e, dc=dc, hh=hh, pa=pa, pb=pb:
                                     e.matmul(bank(2 * pb + hh), wsl[pa][:, dc, :],
                                              xn2[:, dc, 512 * hh:512 * (hh + 1)],
                                              start=(dc == 0), stop=(dc == NDC - 1)),
                                     inc=(tm if (dc == NDC - 1 and hh == 1) else None))
                        wsl_consumed[pa] = tm.n
                        p.w("a", tm, tm.n)
                        p.w("a", sk[pb], kst_stage_free[pb])
                        p.do("a", lambda e, ct=ct, pb=pb:
                             e.activation(out=kst[pb][:, :], in_=bank(2 * pb, 2), func=AF.Identity,
                                          bias=qkb_sb[:, ct:ct + 1]), inc=am)
                        qk_banks_free[pb] = am.n
                        if pb == 1:
                            b23_free[0] = am.n
                        p.w("a", am, am.n)
                        if ct >= 64:
                            h, tgt = ct - 64, vbufT
                        elif ct >= 32:
                            h, tgt = ct - 32, kbuf
                        else:
                            h, tgt = ct, qbuf
                        p.do("a", lambda e, h=h, t0=t0, pb=pb, tgt=tgt:
                             e.dma_start(out=tgt[h, :, t0:t0 + 1024], in_=kst[pb][:, :]), inc=sk[pb], k=16)
                        kst_stage_free[pb] = sk[pb].n
                    scoresT_free[0] = am.n
            qkv_stored = (sk[0].n, sk[1].n)
            p.barrier(bar)

        if "x1" in dump:
            p.do("s", lambda e: e.dma_start(out=dumps["d_x1"], in_=x1T), inc=dmp, k=16)
        if "kvq" in dump:
            p.do("s", lambda e: e.dma_start(out=dumps["d_k"], in_=kbuf), inc=dmp, k=16)
            p.do("s", lambda e: e.dma_start(out=dumps["d_v"], in_=vbufT), inc=dmp, k=16)
            p.do("s", lambda e: e.dma_start(out=dumps["d_q"], in_=qbuf), inc=dmp, k=16)
        if dump:
            p.w("s", dmp, dmp.n)
            p.barrier(bar)

        with ExitStack() as p34:
            ctx_all = p34.enter_context(nc.sbuf_tensor("ctx_all", [128, NH, TOWN], BF16))
            p3 = ExitStack()
            kst3_s = p3.enter_context(nc.sbuf_tensor("kst3", [128, TALL], BF16))
            kst3 = [kst3_s, kst3_s]
            vst3 = [p3.enter_context(nc.sbuf_tensor(f"vst3{i}", [128, TALL], BF16)) for i in range(2)]
            vtT = p3.enter_context(nc.sbuf_tensor("vtT", [128, TALL], BF16))
            qst3 = p3.enter_context(nc.sbuf_tensor("qst3", [128, TOWN], BF16))
            pt = p3.enter_context(nc.sbuf_tensor("pt", [128, 4, 1024], BF16))
            zacc = p3.enter_context(nc.sbuf_tensor("zacc", [128, 1024], F32))
            zb_sb = p3.enter_context(nc.sbuf_tensor("zb_sb", [128, 1024], F32))
            zrow = p3.enter_context(nc.sbuf_tensor("zrow", [1, 1024], F32))
            zrec = p3.enter_context(nc.sbuf_tensor("zrec", [1, 1024], F32))

            kv_consumed = [0, 0]
            k_consumed = [0]
            vt_evac = [0, 0]
            q_consumed = 0
            pt_slots = [(0, 0)] * 4
            st_free = [0, 0]
            ctx_free = 0
            z_banks_free = 0

            vt_done = [0]
            zb_evac = [0]
            p.w("s", sk[0], qkv_stored[0])
            p.w("s", sk[1], qkv_stored[1])
            for h in range(NH):
                pa = h % 2
                p.w("s", tm, k_consumed[0])
                p.do("s", lambda e, h=h, pa=pa: e.dma_start(out=kst3[pa][:, :], in_=kbuf[h, :, :]), inc=k3s, k=16)
                p.w("s", tm, kv_consumed[pa])
                p.w("s", am, vt_done[0])
                p.do("s", lambda e, h=h: e.dma_start(out=vtT[:, :], in_=vbufT[h, :, :]), inc=v3s, k=16)
                p.w("s", tm, q_consumed)
                p.do("s", lambda e, h=h: e.dma_start(out=qst3[:, :], in_=qbuf[h, :, :]), inc=q3s, k=16)
                p.w("t", k3s, k3s.n)
                p.w("t", v3s, v3s.n)
                p.w("t", q3s, q3s.n)
                p.w("t", vm, z_banks_free)
                p.w("t", am, zb_evac[0])
                for g8 in range(8):
                    zbk = 6 + (g8 % 2)
                    if g8 >= 2:
                        p.w("t", am, vt_evac[g8 % 2])
                    for ti in range(4):
                        kt = 4 * g8 + ti
                        p.do("t", lambda e, kt=kt, ti=ti, zbk=zbk:
                             e.matmul(bank(zbk)[:, 128 * ti:128 * (ti + 1)], vtT[:, 128 * kt:128 * (kt + 1)],
                                      id_bf[:, :], start=True, stop=True),
                             inc=(tm if ti == 3 else None))
                    p.w("a", tm, tm.n)
                    p.do("a", lambda e, g8=g8, pa=pa, zbk=zbk:
                         e.activation(out=vst3[pa][:, 512 * g8:512 * (g8 + 1)], in_=bank(zbk), func=AF.Identity),
                         inc=am)
                    vt_evac[g8 % 2] = am.n
                vt_done[0] = am.n
                for qc in range(2):
                    for kt in range(32):
                        stp = kt % 2
                        slot = kt % 4
                        p.w("t", am, st_free[stp])
                        for hh in range(2):
                            p.do("t", lambda e, kt=kt, qc=qc, hh=hh, stp=stp, pa=pa:
                                 e.matmul(bank(2 * stp + hh), kst3[pa][:, 128 * kt:128 * (kt + 1)],
                                          qst3[:, 1024 * qc + 512 * hh:1024 * qc + 512 * (hh + 1)],
                                          start=True, stop=True),
                                 inc=(tm if hh == 1 else None))
                        st_mm = tm.n
                        if qc == 1 and kt == 31:
                            q_consumed = tm.n
                            k_consumed[0] = tm.n
                        p.w("a", tm, st_mm)
                        p.w("a", vm, pt_slots[slot][0])
                        p.w("a", tm, pt_slots[slot][1])
                        p.do("a", lambda e, slot=slot, stp=stp:
                             e.activation(out=pt[:, slot, :], in_=bank(2 * stp, 2), func=AF.Exp,
                                          scale=ISQ, bias=cexp[:, :]), inc=am)
                        st_free[stp] = am.n
                        p.w("v", am, am.n)
                        if kt == 0:
                            p.w("v", vm, ctx_free)
                            p.do("v", lambda e, slot=slot: e.tensor_copy(out=zacc[:, :], in_=pt[:, slot, :]), inc=vm)
                        else:
                            p.w("v", vm, vm.n)
                            p.do("v", lambda e, slot=slot: e.tensor_tensor(out=zacc[:, :], in0=zacc[:, :],
                                                                           in1=pt[:, slot, :], op=ALU.add), inc=vm)
                        zadd = vm.n
                        p.w("t", am, am.n)
                        if kt == 0:
                            p.w("t", vm, ctx_free)
                            p.w("t", am, vt_done[0])
                        for hh in range(2):
                            p.do("t", lambda e, kt=kt, slot=slot, hh=hh, pa=pa:
                                 e.matmul(bank(4 + hh), vst3[pa][:, 128 * kt:128 * (kt + 1)],
                                          pt[:, slot, 512 * hh:512 * (hh + 1)],
                                          start=(kt == 0), stop=(kt == 31)),
                                 inc=(tm if hh == 1 else None))
                        pt_slots[slot] = (zadd, tm.n)
                        if qc == 1 and kt == 31:
                            kv_consumed[pa] = tm.n
                    pv_done = tm.n
                    p.w("t", vm, vm.n)
                    p.w("t", vm, z_banks_free)
                    p.do("t", lambda e: e.matmul(bank(6)[0:1, :], ones_f[:, :], zacc[:, 0:512], start=True, stop=True))
                    p.do("t", lambda e: e.matmul(bank(7)[0:1, :], ones_f[:, :], zacc[:, 512:1024], start=True, stop=True), inc=tm)
                    p.w("v", tm, tm.n)
                    p.do("v", lambda e: e.tensor_copy(out=zrow[0:1, 0:512], in_=bank(6)[0:1, :]))
                    p.do("v", lambda e: e.tensor_copy(out=zrow[0:1, 512:1024], in_=bank(7)[0:1, :]), inc=vm)
                    p.w("v", vm, vm.n)
                    p.do("v", lambda e: e.reciprocal(out=zrec[0:1, :], in_=zrow[0:1, :]), inc=vm)
                    p.w("t", vm, vm.n)
                    p.do("t", lambda e: e.matmul(bank(6), ones1[0:1, :], zrec[0:1, 0:512], start=True, stop=True))
                    p.do("t", lambda e: e.matmul(bank(7), ones1[0:1, :], zrec[0:1, 512:1024], start=True, stop=True), inc=tm)
                    p.w("a", tm, tm.n)
                    p.do("a", lambda e: e.activation(out=zb_sb[:, :], in_=bank(6, 2), func=AF.Identity), inc=am)
                    zb_evac[0] = am.n
                    p.w("v", am, am.n)
                    p.w("v", tm, pv_done)
                    p.do("v", lambda e, h=h, qc=qc: e.tensor_tensor(out=ctx_all[:, h, 1024 * qc:1024 * (qc + 1)],
                                                                    in0=bank(4, 2), in1=zb_sb[:, :], op=ALU.mult), inc=vm)
                    ctx_free = vm.n
                    z_banks_free = vm.n

            p.barrier(bar)
            p3.close()

            with ExitStack() as p4:
                owsl = [p4.enter_context(nc.sbuf_tensor(f"owsl{i}", [128, NDC, 128], BF16)) for i in range(2)]
                x1t = [p4.enter_context(nc.sbuf_tensor(f"x1t{i}", [128, TOWN], F32)) for i in range(2)]
                otmp = [p4.enter_context(nc.sbuf_tensor(f"otmp{i}", [128, TOWN], F32)) for i in range(2)]

                ow_consumed = [0, 0]
                o_banks_free = [0, 0]
                o_res = [0, 0]
                o_st = [0, 0]
                ctx_ready = vm.n
                p.w("s", sx[0], sx[0].n)
                p.w("s", sx[1], sx[1].n)
                for ct in range(32):
                    pa = ct % 2
                    p.w("s", tm, ow_consumed[pa])
                    p.do("s", lambda e, ct=ct, pa=pa:
                         e.dma_start(out=owsl[pa][:, :, :],
                                     in_=ow_bf.rearrange("(dc p) c -> p dc c", p=128)[:, :, 128 * ct:128 * (ct + 1)]),
                         inc=wo[pa], k=16)
                    p.w("s", vm, o_res[pa])
                    p.do("s", lambda e, ct=ct, pa=pa:
                         e.dma_start(out=x1t[pa][:, :], in_=x1T[128 * ct:128 * (ct + 1), 0:TOWN]), inc=xl[pa], k=16)
                    p.w("t", wo[pa], wo[pa].n)
                    p.w("t", vm, ctx_ready)
                    p.w("t", am, o_banks_free[pa])
                    for hh in range(NH):
                        for q4 in range(4):
                            p.do("t", lambda e, hh=hh, q4=q4, pa=pa:
                                 e.matmul(bank(4 * pa + q4), owsl[pa][:, hh, :], ctx_all[:, hh, 512 * q4:512 * (q4 + 1)],
                                          start=(hh == 0), stop=(hh == NH - 1)),
                                 inc=(tm if (hh == NH - 1 and q4 == 3) else None))
                    ow_consumed[pa] = tm.n
                    p.w("a", tm, tm.n)
                    p.w("a", vm, o_res[pa])
                    p.do("a", lambda e, ct=ct, pa=pa: e.activation(out=otmp[pa][:, :], in_=bank(4 * pa, 4),
                                                                   func=AF.Identity, bias=ob_sb[:, ct:ct + 1]), inc=am)
                    o_banks_free[pa] = am.n
                    p.w("v", am, am.n)
                    p.w("v", xl[pa], xl[pa].n)
                    p.w("v", s2[pa], o_st[pa])
                    p.do("v", lambda e, pa=pa: e.tensor_tensor(out=otmp[pa][:, :], in0=otmp[pa][:, :],
                                                               in1=x1t[pa][:, :], op=ALU.add), inc=vm)
                    o_res[pa] = vm.n
                    p.w("a", vm, vm.n)
                    p.do("a", lambda e, ct=ct, pa=pa:
                         e.dma_start(out=x2T[128 * ct:128 * (ct + 1), :], in_=otmp[pa][:, :]), inc=s2[pa], k=16)
                    o_st[pa] = s2[pa].n
                p.w("a", s2[0], s2[0].n)
                p.w("a", s2[1], s2[1].n)
                p.barrier(bar)
            x2_stored = (s2[0].n, s2[1].n)

        if "x2" in dump:
            p.do("s", lambda e: e.dma_start(out=dumps["d_x2"], in_=x2T), inc=dmp, k=16)
            p.w("s", dmp, dmp.n)
            p.barrier(bar)

        with ExitStack() as p5:
            buf["xf"] = [p5.enter_context(nc.sbuf_tensor(f"xf5_{i}", [128, NDC, 128], F32)) for i in range(2)]
            buf["xb2"] = [p5.enter_context(nc.sbuf_tensor(f"xb5_{i}", [128, NDC, 128], BF16)) for i in range(2)]
            buf["xq2"] = p5.enter_context(nc.sbuf_tensor("xq25", [128, NDC, 128], BF16))
            buf["mrow"] = p5.enter_context(nc.sbuf_tensor("mrow5", [1, 4, 128], F32))
            buf["rrow"] = p5.enter_context(nc.sbuf_tensor("rrow5", [1, 4, 128], F32))
            buf["qrow"] = p5.enter_context(nc.sbuf_tensor("qrow5", [1, 2, 128], F32))
            buf["trow"] = p5.enter_context(nc.sbuf_tensor("trow5", [1, 2, 128], F32))
            buf["hg"] = p5.enter_context(nc.sbuf_tensor("hg5", [128, 8, 512], BF16))
            buf["fslab"] = [p5.enter_context(nc.sbuf_tensor(f"fslab5{i}", [128, 4096], BF16)) for i in range(2)]
            buf["xres"] = [p5.enter_context(nc.sbuf_tensor(f"xres5{i}", [128, 512], F32)) for i in range(2)]
            buf["xout"] = [p5.enter_context(nc.sbuf_tensor(f"xout5{i}", [128, 512], F32)) for i in range(2)]
            buf["snat"] = p5.enter_context(nc.sbuf_tensor("snat5", [128, 4, 64], F32))
            buf["sT_sb"] = p5.enter_context(nc.sbuf_tensor("sT_sb5", [64, 512], F32))
            buf["GT_sb"] = p5.enter_context(nc.sbuf_tensor("GT_sb5", [64, 512], F32))
            buf["gx_sb"] = p5.enter_context(nc.sbuf_tensor("gx_sb5", [128, 512], BF16))
            xn5 = p5.enter_context(nc.sbuf_tensor("xn5", [128, NDC, 512], BF16))

            src_ready[0] = [(s2[0], x2_stored[0]), (s2[1], x2_stored[1])]
            p.w("s", s2[0], x2_stored[0])
            p.w("s", s2[1], x2_stored[1])
            for w5 in range(4):
                t0 = 512 * w5
                emit_ln(x2T, t0, xn5, 0)
                emit_ffn(xn5, t0, x2T, outT, t0)

        p.w("s", sx[0], sx[0].n)
        p.w("s", sx[1], sx[1].n)
        p.barrier(bar)
        p.flush()
    return nc, dumps


_NC_CACHE = {}


def _get_nc(dump=()):
    key = tuple(sorted(dump))
    if key not in _NC_CACHE:
        _NC_CACHE[key] = build(dump)
    return _NC_CACHE[key]


def _host_inputs(L0_attn_x, L0_attn_attn_qkvw, L0_attn_attn_qkvb, L0_attn_attn_ow,
                 L0_attn_attn_ob, L0_ffn_A, L0_ffn_B, L0_ffn_Wr):
    x = np.asarray(L0_attn_x, np.float32)
    wq = np.ascontiguousarray(np.asarray(L0_attn_attn_qkvw, np.float32))
    qkvb = np.asarray(L0_attn_attn_qkvb, np.float32)
    owh = np.ascontiguousarray(np.asarray(L0_attn_attn_ow, np.float32))
    obh = np.asarray(L0_attn_attn_ob, np.float32)
    A = np.asarray(L0_ffn_A, np.float32)
    B = np.asarray(L0_ffn_B, np.float32)
    Wr = np.asarray(L0_ffn_Wr, np.float32)

    wrT = np.ascontiguousarray(Wr.T)
    aTh = np.ascontiguousarray(np.transpose(A, (2, 0, 1)).reshape(D, 1024))
    bfh = np.ascontiguousarray(np.transpose(B, (0, 2, 1)).reshape(1024, D))
    qkbh = np.ascontiguousarray(qkvb.reshape(96, 128).T)
    obrh = np.ascontiguousarray(obh.reshape(32, 128).T)
    em = np.zeros((64, 1024), np.float32)
    em[np.arange(1024) >> 4, np.arange(1024)] = 1.0
    idh = np.eye(128, dtype=np.float32)

    shared = dict(wqkv=wq, ow=owh, wrT=wrT, aT=aTh, bfm=bfh, qkb=qkbh,
                  obr=obrh, emat=em, idf=idh)
    in_maps = []
    for c in range(8):
        b, hf = c // 2, c % 2
        own = x[b, 2048 * hf:2048 * (hf + 1), :]
        sib = x[b, 2048 * (1 - hf):2048 * (2 - hf), :]
        xTh = np.ascontiguousarray(np.concatenate([own, sib], 0).T)
        in_maps.append(dict(shared, xT=xTh))
    return in_maps


def kernel(L0_attn_x, L0_attn_attn_qkvw, L0_attn_attn_qkvb, L0_attn_attn_ow,
           L0_attn_attn_ob, L0_ffn_A, L0_ffn_B, L0_ffn_Wr):
    in_maps = _host_inputs(L0_attn_x, L0_attn_attn_qkvw, L0_attn_attn_qkvb,
                           L0_attn_attn_ow, L0_attn_attn_ob, L0_ffn_A, L0_ffn_B, L0_ffn_Wr)
    nc, _ = _get_nc()
    res = run_bass_kernel_spmd(nc, in_maps, core_ids=list(range(8)))
    out = np.empty((4, 4096, 4096), np.float32)
    for c in range(8):
        b, hf = c // 2, c % 2
        out[b, 2048 * hf:2048 * (hf + 1), :] = res.results[c]["outT"].T
    return out
